# version 10
# baseline (speedup 1.0000x reference)
import numpy as np
import ml_dtypes
from contextlib import ExitStack

import concourse.bass as bass
import concourse.mybir as mybir
from concourse.bass_utils import run_bass_kernel_spmd

F32 = mybir.dt.float32
BF16 = mybir.dt.bfloat16
ALU = mybir.AluOpType
ACT = mybir.ActivationFunctionType
AX = mybir.AxisListType

T = 2048
D = 1024
HID = 4096
NH = 16
DK = 64
NE = 8
DC = D // 128
HC = HID // 128
NTT = T // 128
QB = 512
NQB = T // QB
HALF = T // 2
FQB = 128
NFQB_H = HALF // FQB
D1 = 256

NEG_TIE = -1e30


def _host_consts():
    c = {}
    ident = np.zeros((128, 128), dtype=ml_dtypes.bfloat16)
    np.fill_diagonal(ident, ml_dtypes.bfloat16(1.0))
    c["c_ident"] = ident
    identf = np.zeros((128, 128), dtype=np.float32)
    np.fill_diagonal(identf, 1.0)
    c["c_identf"] = identf
    masks = np.zeros((128, 4, 512), dtype=ml_dtypes.bfloat16)
    for j in range(4):
        k = np.arange(128)[:, None]
        q = np.arange(512)[None, :]
        masks[:, j, :] = (q >= j * 128 + k).astype(ml_dtypes.bfloat16)
    c["c_masks"] = np.ascontiguousarray(masks.reshape(128, 4 * 512))
    inv_freq = (1.0 / (np.float32(10000.0) ** (np.arange(0, DK, 2, dtype=np.float32) / np.float32(DK)))).astype(np.float32)
    tpos = np.arange(T, dtype=np.float32)
    ang = (tpos[:, None] * inv_freq[None, :]).astype(np.float32)
    cos = np.cos(ang).astype(np.float32)
    sin = np.sin(ang).astype(np.float32)
    A = np.concatenate([cos.T, cos.T, cos.T, cos.T], axis=0)
    Bt = np.concatenate([-sin.T, sin.T, -sin.T, sin.T], axis=0)
    c["c_ropeA"] = np.ascontiguousarray(A, dtype=np.float32)
    c["c_ropeB"] = np.ascontiguousarray(Bt, dtype=np.float32)
    return c


class Chan:
    def __init__(self, es, nc, name, dma=False, slotted=False):
        self.sem = es.enter_context(nc.semaphore(name))
        self.n = 0
        self.name = name
        self.dma = dma
        self.slotted = slotted
        self.maxwaited = 0

    def tick(self):
        self.n += 16 if self.dma else 1
        return self.n


class Res:
    __slots__ = ("wt", "rd", "name")

    def __init__(self, name=""):
        self.wt = {}
        self.rd = {}
        self.name = name


class Emit:
    ENGINES = ("sync", "tensor", "vector", "scalar", "gpsimd")

    def __init__(self, block, chans):
        self.block = block
        self.ch = chans
        self.qs = {e: [] for e in self.ENGINES}
        self.lastw = {e: {} for e in self.ENGINES}

    def _wait_list(self, eng, toks):
        wl = []
        for (ch, n) in sorted(toks, key=lambda t: t[0].name):
            if n <= 0:
                continue
            assert n <= ch.n, f"wait {ch.name}>={n} but only {ch.n} emitted"
            if ch.dma and not ch.slotted:
                n = ch.n
            if ch.dma:
                ch.maxwaited = max(ch.maxwaited, n)
            if self.lastw[eng].get(id(ch), -1) >= n:
                continue
            self.lastw[eng][id(ch)] = n
            wl.append((ch.sem, n))
        return wl

    def op(self, eng, method, *args, reads=(), writes=(), chan=None,
           psum_start=True, psum_stop=True, extra_waits=(), **kw):
        ch = chan if chan is not None else self.ch[eng]
        toks = list(extra_waits)
        if ch.dma and ch.maxwaited > 0:
            toks.append((ch, min(ch.maxwaited, ch.n)))
        for r in reads:
            toks += list(r.wt.items())
        for w in writes:
            toks += list(w.wt.items())
            toks += list(w.rd.items())
        wl = self._wait_list(eng, toks)
        n = ch.tick()
        tok = (ch, n)
        for r in reads:
            r.rd[ch] = n
        for w in writes:
            if psum_stop:
                w.wt = {ch: n}
                w.rd = {}
        def fn(e, method=method, args=args, kw=kw, wl=wl, ch=ch):
            for (sem, v) in wl:
                e.wait_ge(sem, v)
            inst = getattr(e, method)(*args, **kw)
            inst.then_inc(ch.sem, 16 if ch.dma else 1)
            return inst
        self.qs[eng].append(fn)
        return tok

    def mm(self, psum_res, out_ap, lhsT, rhs, reads, start, stop, **kw):
        return self.op(
            "tensor", "matmul", out_ap, lhsT, rhs,
            reads=reads,
            writes=(psum_res,) if (start or stop) else (psum_res,),
            psum_stop=stop,
            start=start, stop=stop,
            **kw) if start else self._mm_mid(psum_res, out_ap, lhsT, rhs, reads, stop, **kw)

    def _mm_mid(self, psum_res, out_ap, lhsT, rhs, reads, stop, **kw):
        ch = self.ch["tensor"]
        toks = []
        for r in reads:
            toks += list(r.wt.items())
        wl = self._wait_list("tensor", toks)
        n = ch.tick()
        for r in reads:
            r.rd[ch] = n
        if stop:
            psum_res.wt = {ch: n}
            psum_res.rd = {}
        def fn(e, wl=wl, out_ap=out_ap, lhsT=lhsT, rhs=rhs, kw=kw, stop=stop, ch=ch):
            for (sem, v) in wl:
                e.wait_ge(sem, v)
            inst = e.matmul(out_ap, lhsT, rhs, start=False, stop=stop, **kw)
            inst.then_inc(ch.sem, 1)
            return inst
        self.qs["tensor"].append(fn)
        return (ch, n)

    def wait_toks(self, eng, toks):
        wl = self._wait_list(eng, toks)
        if wl:
            self.qs[eng].append(lambda e, wl=wl: [e.wait_ge(s, v) for (s, v) in wl])

    def barrier_tok(self):
        seen = {}
        for chs in self._all_chans:
            seen[chs] = chs.n
        return [(c, n) for c, n in seen.items()]

    def flush(self):
        for eng in self.ENGINES:
            fns = self.qs[eng]
            if fns:
                getattr(self.block, eng)(lambda e, fns=tuple(fns): [f(e) for f in fns])
        self.qs = {e: [] for e in self.ENGINES}
        self.lastw = {e: {} for e in self.ENGINES}


class MK:
    def __init__(self, debug_taps=(), upto=None):
        self.debug_taps = list(debug_taps)
        self.upto = upto
        self.tap_aps = {}

    def build(self):
        nc = bass.Bass("TRN2", target_bir_lowering=False, debug=False)
        self.nc = nc
        es = ExitStack()
        self.es = es

        d = {}
        d["x"] = nc.dram_tensor("x", [T, D], F32, kind="ExternalInput").ap()
        for nm, shp in [
            ("L0_attn_Wq", [D, D]), ("L0_attn_Wk", [D, D]), ("L0_attn_Wv", [D, D]),
            ("L0_attn_Wo", [D, D]),
            ("L0_attn_bq", [D]), ("L0_attn_bk", [D]), ("L0_attn_bv", [D]), ("L0_attn_bo", [D]),
            ("L0_ffn_w1", [HID, D]), ("L0_ffn_w2", [D, HID]),
            ("L0_ffn_conv_w", [HID, 1, 3]), ("L0_ffn_conv_b", [HID]),
            ("L1_attn_Wq", [D, D1]), ("L1_attn_Wk", [D, D1]),
            ("L1_attn_Wvdown", [D, D1]), ("L1_attn_Wvup", [D1, D]),
            ("L1_ffn_rw1", [D, HID]), ("L1_ffn_rb1", [HID]),
            ("L1_ffn_rw2", [HID, NE]), ("L1_ffn_rb2", [NE]),
            ("L1_ffn_w1", [NE, D, HID]), ("L1_ffn_b1", [NE, HID]),
            ("L1_ffn_w2", [NE, HID, D]), ("L1_ffn_b2", [NE, D]),
            ("c_ropeA", [128, T]), ("c_ropeB", [128, T]),
        ]:
            d[nm] = nc.dram_tensor(nm, shp, F32, kind="ExternalInput").ap()
        for nm, shp in [("Wq_p", [D, D]), ("Wk_p", [D, D]), ("bq_p", [D]), ("bk_p", [D]),
                        ("rw1_p", [128, HC * DC * 128]), ("w1e_p", [NE, 128, HC * DC * 128])]:
            d[nm] = nc.dram_tensor(nm, shp, F32, kind="ExternalInput").ap()
        d["c_ident"] = nc.dram_tensor("c_ident", [128, 128], BF16, kind="ExternalInput").ap()
        d["c_identf"] = nc.dram_tensor("c_identf", [128, 128], F32, kind="ExternalInput").ap()
        d["c_masks"] = nc.dram_tensor("c_masks", [128, 4 * 512], BF16, kind="ExternalInput").ap()
        d["out"] = nc.dram_tensor("out", [T, D], F32, kind="ExternalOutput").ap()
        d["w1qT_d"] = nc.dram_tensor("w1qT_d", [128, (HID // 128) * D], BF16, kind="Internal").ap()
        d["w1e_b"] = nc.dram_tensor("w1e_b", [NE, 128, HC * DC * 128], BF16, kind="Internal").ap()
        d["w2e_b"] = nc.dram_tensor("w2e_b", [NE, HID, D], BF16, kind="Internal").ap()
        d["w2qT_d"] = nc.dram_tensor("w2qT_d", [HID, D], BF16, kind="Internal").ap()
        d["WoT_d"] = nc.dram_tensor("WoT_d", [D, D], BF16, kind="Internal").ap()
        self.d = d

        self._uid = [0]
        def _uniq(name):
            self._uid[0] += 1
            return f"{name}__{self._uid[0]}"
        _orig_sb = nc.sbuf_tensor
        _orig_ps = nc.psum_tensor
        nc.sbuf_tensor = lambda name, shape, dt: _orig_sb(_uniq(name), shape, dt)
        nc.psum_tensor = lambda name, shape, dt: _orig_ps(_uniq(name), shape, dt)
        sbt = lambda name, shape, dt: es.enter_context(nc.sbuf_tensor(name, shape, dt))
        self.sbt = sbt
        self.x = sbt("x_tm", [128, NTT * D], F32)
        self.ident = sbt("ident", [128, 128], BF16)
        self.identf = sbt("identf", [128, 128], F32)
        self.masks = sbt("masks", [128, 4 * 512], BF16)
        self.mcols = sbt("mcols", [128, NTT], F32)
        self.icols = sbt("icols", [128, NTT], F32)
        self.gcols = sbt("gcols", [128, 8], F32)
        self.cxcols = sbt("cxcols", [128, NTT], F32)
        self.ones_row_bf = sbt("ones_row_bf", [1, 128], BF16)
        self.ones_row_f = sbt("ones_row_f", [64, 128], F32)
        self.ones_col_bf = sbt("ones_col_bf", [128, 1], BF16)
        self.ones_col_f = sbt("ones_col_f", [128, 1], F32)

        self.cT = Chan(es, nc, "cT")
        self.cV = Chan(es, nc, "cV")
        self.cA = Chan(es, nc, "cA")
        self.cP = Chan(es, nc, "cP")
        self.cS = Chan(es, nc, "cS", dma=True)
        self.cW = [Chan(es, nc, f"cW{i}", dma=True, slotted=True) for i in range(6)]
        self.cG = Chan(es, nc, "cG", dma=True)
        self._dma_chans = [self.cS, self.cG] + self.cW

        self.block = es.enter_context(nc.Block())
        em = Emit(self.block, {"tensor": self.cT, "vector": self.cV,
                               "scalar": self.cA, "gpsimd": self.cP,
                               "sync": self.cS})
        em._all_chans = [self.cT, self.cV, self.cA, self.cP, self.cS, self.cG] + self.cW
        self.em = em

        self._res = {}
        self.R = lambda *key: self._res.setdefault(key, Res(str(key)))

        with es:
            self._emit_all()
        return nc

    def mp_init(self):
        d = self.d
        self._mp = []
        self._mp_i = 0
        self._mp_sid = 0
        for e in range(NE):
            for k in range(HC):
                self._mp.append((d["w1e_b"][e][:, k * 1024:(k + 1) * 1024],
                                 d["w1e_p"][e][:, k * 1024:(k + 1) * 1024]))
            for m in range(HC):
                self._mp.append((d["w2e_b"][e][m * 128:(m + 1) * 128, :],
                                 d["L1_ffn_w2"][e, m * 128:(m + 1) * 128, :]))

    def mp_scope(self, ss, nbuf=2):
        nc, R = self.nc, self.R
        self._mp_sid += 1
        sid = self._mp_sid
        stg = [ss.enter_context(nc.sbuf_tensor(f"mp_stg{k}", [128, 1024], F32)) for k in range(nbuf)]
        bf = [ss.enter_context(nc.sbuf_tensor(f"mp_bf{k}", [128, 1024], BF16)) for k in range(nbuf)]
        return {"stg": stg, "bf": bf, "nbuf": nbuf, "pend": [],
                "rstg": [R("mp", "stg", sid, k) for k in range(nbuf)],
                "rbf": [R("mp", "bf", sid, k) for k in range(nbuf)]}

    def _mp_cast(self, st, i, j):
        dst = self._mp[i][0]
        if i % 2 == 0:
            self.em.op("vector", "tensor_copy", st["bf"][j][:, :], st["stg"][j][:, :],
                       reads=(st["rstg"][j],), writes=(st["rbf"][j],))
        else:
            self.em.op("scalar", "copy", st["bf"][j][:, :], st["stg"][j][:, :],
                       reads=(st["rstg"][j],), writes=(st["rbf"][j],))
        self.dma_out(dst, st["bf"][j][:, :], reads=(st["rbf"][j],),
                     writes=(self.R("mp", "dram"),))

    def mp_step(self, st, n):
        if st is None:
            return
        for _ in range(min(n, len(st["pend"]))):
            self._mp_cast(st, *st["pend"].pop(0))
        free = st["nbuf"] - len(st["pend"])
        for _ in range(min(n, free)):
            if self._mp_i >= len(self._mp):
                return
            i = self._mp_i
            self._mp_i += 1
            j = i % st["nbuf"]
            self.dma_in(st["stg"][j][:, :], self._mp[i][1], writes=(st["rstg"][j],),
                        chan=self.cW[4 + (i % 2)])
            st["pend"].append((i, j))

    def mp_flush(self, st):
        if st is None:
            return
        while st["pend"]:
            self._mp_cast(st, *st["pend"].pop(0))

    def barrier(self):
        toks = self.em.barrier_tok()
        for eng in Emit.ENGINES:
            self.em.wait_toks(eng, toks)

    def dma_in(self, dst_ap, src_ap, writes, reads=(), chan=None, extra_waits=(),
               slow=False):
        kw = {"allow_slow_non_contiguous": True} if slow else {}
        return self.em.op("sync", "dma_start", chan=chan or self.cS,
                          reads=reads, writes=writes, extra_waits=extra_waits,
                          out=dst_ap, in_=src_ap, **kw)

    def dma_out(self, dst_ap, src_ap, reads, writes=(), extra_waits=()):
        return self.em.op("gpsimd", "dma_start", chan=self.cG,
                          reads=reads, writes=writes, extra_waits=extra_waits,
                          out=dst_ap, in_=src_ap)

    def _emit_all(self):
        self.mp_init()
        self.phase_load()
        if self.upto == "load":
            return self.phase_out()
        self.phase_stats(0)
        if self.upto == "stats0":
            return self.phase_out()
        self.phase_wprep()
        if self.upto == "wprep":
            return self.phase_out()
        self.phase_l0()
        if self.upto == "l0":
            return self.phase_out()
        self.phase_stats(1)
        self.phase_l1()
        self.phase_out()

    def phase_load(self):
        d, em = self.d, self.em
        xin = d["x"].rearrange("(n p) d -> p n d", p=128)
        xdst = self.x[:, :].rearrange("p (n d) -> p n d", d=D)
        self.dma_in(xdst, xin, writes=(self.R("x"),))
        self.dma_in(self.ident[:, :], d["c_ident"][:, :], writes=(self.R("ident"),))
        self.dma_in(self.identf[:, :], d["c_identf"][:, :], writes=(self.R("ident"),))
        self.dma_in(self.masks[:, :], d["c_masks"][:, :], writes=(self.R("masks"),))
        em.op("vector", "memset", self.gcols[:, :], 0.0, writes=(self.R("gcols"),))
        em.op("vector", "memset", self.ones_row_bf[:, :], 1.0, writes=(self.R("ones"),))
        em.op("vector", "memset", self.ones_row_f[:, :], 1.0, writes=(self.R("ones"),))
        em.op("vector", "memset", self.ones_col_bf[:, :], 1.0, writes=(self.R("ones"),))
        em.op("vector", "memset", self.ones_col_f[:, :], 1.0, writes=(self.R("ones"),))
        em.flush()

    def phase_stats(self, layer):
        nc, em, R = self.nc, self.em, self.R
        with ExitStack() as ss:
            cen = ss.enter_context(nc.sbuf_tensor(f"st_cen{layer}", [128, D], F32))
            sq = ss.enter_context(nc.sbuf_tensor(f"st_sq{layer}", [128, D], F32))
            ssm = ss.enter_context(nc.sbuf_tensor(f"st_ss{layer}", [128, NTT], F32))
            tmp = ss.enter_context(nc.sbuf_tensor(f"st_tmp{layer}", [128, NTT], F32))
            rcen, rsq, rss, rtmp = R("st_cen"), R("st_sq"), R("st_ss"), R("st_tmp")
            rm, ri = R("mcols"), R("icols")
            for t in range(NTT):
                xs = self.x[:, t * D:(t + 1) * D]
                em.op("vector", "reduce_sum", tmp[:, t:t + 1], xs, AX.X,
                      reads=(R("x"),), writes=(rtmp,))
                em.op("vector", "tensor_scalar", self.mcols[:, t:t + 1], tmp[:, t:t + 1],
                      1.0 / D, None, ALU.mult, reads=(rtmp,), writes=(rm,))
                em.op("vector", "tensor_scalar", cen[:, :], xs, self.mcols[:, t:t + 1],
                      None, ALU.subtract, reads=(R("x"), rm), writes=(rcen,))
                em.op("scalar", "activation", sq[:, :], cen[:, :], ACT.Square,
                      reads=(rcen,), writes=(rsq, rss), accum_out=ssm[:, t:t + 1])
            em.op("vector", "tensor_scalar", ssm[:, :], ssm[:, :], 1.0 / D, 1e-5,
                  ALU.mult, ALU.add, reads=(rss,), writes=(rss,))
            em.op("vector", "reciprocal", tmp[:, :], ssm[:, :], reads=(rss,), writes=(rtmp,))
            em.op("scalar", "activation", self.icols[:, :], tmp[:, :], ACT.Sqrt,
                  reads=(rtmp,), writes=(ri,))
            em.op("vector", "tensor_mul", tmp[:, :], self.icols[:, :], self.icols[:, :],
                  reads=(ri,), writes=(rtmp,))
            em.op("vector", "tensor_mul", tmp[:, :], tmp[:, :], ssm[:, :],
                  reads=(rtmp, rss), writes=(rtmp,))
            em.op("vector", "tensor_scalar", tmp[:, :], tmp[:, :], -0.5, 1.5,
                  ALU.mult, ALU.add, reads=(rtmp,), writes=(rtmp,))
            em.op("vector", "tensor_mul", self.icols[:, :], self.icols[:, :], tmp[:, :],
                  reads=(ri, rtmp), writes=(ri,))
            self.barrier()
            em.flush()
        self.tap("stats_m", self.mcols)
        self.tap("stats_i", self.icols)

    def emit_xn_tile(self, t, dst, dst_res):
        self.em.op("vector", "tensor_scalar", dst[:, :],
                   self.x[:, t * D:(t + 1) * D],
                   self.mcols[:, t:t + 1], self.icols[:, t:t + 1],
                   ALU.subtract, ALU.mult,
                   reads=(self.R("x"), self.R("mcols"), self.R("icols")),
                   writes=(dst_res,))

    def emit_transpose_tile(self, src, src_res, dst_cols, ps, psr, dtype_bf=True):
        em = self.em
        for c in range(DC):
            pr = psr[c % 2]
            em.op("tensor", "transpose", ps[c % 2][:, 0:128], src[:, c * 128:(c + 1) * 128],
                  self.ident[:, :], reads=(src_res, self.R("ident")), writes=(pr,))
            dap, dres = dst_cols(c)
            em.op("vector", "tensor_copy", dap, ps[c % 2][:, 0:128],
                  reads=(pr,), writes=(dres,))

    def _build_wT(self, tag, src, dst, O, I, sca_col=None, ss=None, pm=False, mp=None):
        nc, em, R = self.nc, self.em, self.R
        stg = [ss.enter_context(nc.sbuf_tensor(f"{tag}_st{k}", [128, I], F32)) for k in range(2)]
        wq = [ss.enter_context(nc.sbuf_tensor(f"{tag}_wq{k}", [128, I], BF16)) for k in range(2)]
        tb = [ss.enter_context(nc.sbuf_tensor(f"{tag}_tb{k}", [128, 128], BF16)) for k in range(4)]
        gl = ss.enter_context(nc.sbuf_tensor(f"{tag}_g", [128, I], F32))
        ps = [ss.enter_context(nc.psum_tensor(f"{tag}_ps{k}", [128, 1024], BF16)) for k in range(2)]
        rstg = [R(tag, "stg", k) for k in range(2)]
        rwq = [R(tag, "wq", k) for k in range(2)]
        rtb = [R(tag, "tb", k) for k in range(4)]
        rgl = R(tag, "gl")
        rps = [R(tag, "ps", k) for k in range(2)]
        rdst = R(tag, "dram")
        ti = 0
        for oo in range(O // 128):
            p = oo % 2
            self.mp_step(mp, 2)
            self.dma_in(stg[p][:, :], src[oo * 128:(oo + 1) * 128, :], writes=(rstg[p],))
            if sca_col is not None:
                em.op("vector", "tensor_scalar", gl[:, :], stg[p][:, :], sca_col, 0.5,
                      ALU.mult, ALU.is_gt, reads=(rstg[p], R("gcols")), writes=(rgl,))
                em.op("vector", "tensor_scalar", stg[p][:, :], stg[p][:, :], sca_col, -0.5,
                      ALU.mult, ALU.is_lt, reads=(rstg[p], R("gcols")), writes=(rstg[p],))
                em.op("vector", "tensor_sub", wq[p][:, :], gl[:, :], stg[p][:, :],
                      reads=(rgl, rstg[p]), writes=(rwq[p],))
            else:
                em.op("vector", "tensor_copy", wq[p][:, :], stg[p][:, :],
                      reads=(rstg[p],), writes=(rwq[p],))
            for ii in range(I // 128):
                pp = ti % 2
                tt = ti % 4
                em.op("tensor", "transpose", ps[pp][:, 0:128], wq[p][:, ii * 128:(ii + 1) * 128],
                      self.ident[:, :], reads=(rwq[p], R("ident")), writes=(rps[pp],))
                em.op("scalar", "copy", tb[tt][:, :], ps[pp][:, 0:128],
                      reads=(rps[pp],), writes=(rtb[tt],))
                if pm:
                    dslc = dst[:, (oo * (I // 128) + ii) * 128:(oo * (I // 128) + ii + 1) * 128]
                else:
                    dslc = dst[ii * 128:(ii + 1) * 128, oo * 128:(oo + 1) * 128]
                self.dma_out(dslc, tb[tt][:, :], reads=(rtb[tt],), writes=(rdst,))
                ti += 1

    def _bitnet_scale(self, tag, src, O, I, gi, ss, mp=None):
        nc, em, R = self.nc, self.em, self.R
        stg = [ss.enter_context(nc.sbuf_tensor(f"{tag}_ms{k}", [128, I], F32)) for k in range(2)]
        col = ss.enter_context(nc.sbuf_tensor(f"{tag}_mc", [128, 2], F32))
        rstg = [R(tag, "ms", k) for k in range(2)]
        rcol = R(tag, "mc")
        for oo in range(O // 128):
            p = oo % 2
            self.mp_step(mp, 1)
            self.dma_in(stg[p][:, :], src[oo * 128:(oo + 1) * 128, :], writes=(rstg[p],))
            em.op("vector", "reduce_sum", col[:, 1:2], stg[p][:, :], AX.X,
                  apply_absolute_value=True, reads=(rstg[p],), writes=(rcol,))
            if oo == 0:
                em.op("vector", "tensor_copy", col[:, 0:1], col[:, 1:2],
                      reads=(rcol,), writes=(rcol,))
            else:
                em.op("vector", "tensor_add", col[:, 0:1], col[:, 0:1], col[:, 1:2],
                      reads=(rcol,), writes=(rcol,))
        ps = ss.enter_context(nc.psum_tensor(f"{tag}_ps", [128, 2], F32))
        rps = R(tag, "ps")
        em.op("tensor", "matmul", ps[0:1, 0:1], self.ones_col_f[:, 0:1], col[:, 0:1],
              reads=(R("ones"), rcol), writes=(rps,), start=True, stop=True)
        em.op("vector", "tensor_copy", col[0:1, 1:2], ps[0:1, 0:1],
              reads=(rps,), writes=(rcol,))
        em.op("tensor", "matmul", ps[:, 1:2], self.ones_row_f[0:1, :], col[0:1, 1:2],
              reads=(R("ones"), rcol), writes=(rps,), start=True, stop=True)
        em.op("vector", "tensor_copy", col[:, 0:1], ps[:, 1:2],
              reads=(rps,), writes=(rcol,))
        em.op("vector", "tensor_scalar", col[:, 0:1], col[:, 0:1], 1.0 / (O * I), 1e-5,
              ALU.mult, ALU.max, reads=(rcol,), writes=(rcol,))
        em.op("vector", "reciprocal", self.gcols[:, gi:gi + 1], col[:, 0:1],
              reads=(rcol,), writes=(self.R("gcols"),))
        em.op("vector", "reciprocal", self.gcols[:, gi + 1:gi + 2], self.gcols[:, gi:gi + 1],
              reads=(self.R("gcols"),), writes=(self.R("gcols"),))

    def phase_wprep(self):
        d = self.d
        with ExitStack() as ss:
            mp = self.mp_scope(ss)
            self._bitnet_scale("m1", d["L0_ffn_w1"], HID, D, 0, ss, mp=mp)
            self._bitnet_scale("m2", d["L0_ffn_w2"], D, HID, 2, ss, mp=mp)
            self.mp_flush(mp)
            self.barrier()
        with ExitStack() as ss:
            mp = self.mp_scope(ss)
            self._build_wT("wo", d["L0_attn_Wo"], d["WoT_d"], D, D, None, ss, mp=mp)
            self.mp_flush(mp)
            self.barrier()
        with ExitStack() as ss:
            mp = self.mp_scope(ss)
            self._build_wT("w1", d["L0_ffn_w1"], d["w1qT_d"], HID, D,
                           self.gcols[:, 0:1], ss, pm=True, mp=mp)
            self.mp_flush(mp)
            self.barrier()
        with ExitStack() as ss:
            mp = self.mp_scope(ss)
            self._build_wT("w2", d["L0_ffn_w2"], d["w2qT_d"], D, HID,
                           self.gcols[:, 2:3], ss, mp=mp)
            self.mp_flush(mp)
            self.barrier()
        self.em.flush()
        self.tap("gcols", self.gcols)


    def mm(self, psum_res, out_ap, lhsT_ap, rhs_ap, reads, start, stop, **kw):
        if start:
            return self.em.op("tensor", "matmul", out_ap, lhsT_ap, rhs_ap,
                              reads=tuple(reads), writes=(psum_res,),
                              psum_stop=stop, start=True, stop=stop, **kw)
        return self.em._mm_mid(psum_res, out_ap, lhsT_ap, rhs_ap, tuple(reads), stop, **kw)

    def _load_wT_sbuf(self, tag, src, dst, dst_res, O, I, ss, nbuf=2, stg_ext=None):
        nc, em, R = self.nc, self.em, self.R
        if stg_ext is not None:
            nbuf = 1
        stg = ([stg_ext[0]] if stg_ext is not None else
               [ss.enter_context(nc.sbuf_tensor(f"{tag}_st{k}", [128, I], F32)) for k in range(nbuf)])
        wb = [ss.enter_context(nc.sbuf_tensor(f"{tag}_wb{k}", [128, I], BF16)) for k in range(nbuf)]
        ps = [ss.enter_context(nc.psum_tensor(f"{tag}_ps{k}", [128, 1024], BF16)) for k in range(2)]
        rstg = ([stg_ext[1]] if stg_ext is not None else
                [R(tag, "stg", k) for k in range(nbuf)])
        rwb = [R(tag, "wb", k) for k in range(nbuf)]
        rps = [R(tag, "ps", k) for k in range(2)]
        ti = 0
        for oo in range(O // 128):
            p = oo % nbuf
            self.dma_in(stg[p][:, :], src[oo * 128:(oo + 1) * 128, :], writes=(rstg[p],))
            em.op("vector", "tensor_copy", wb[p][:, :], stg[p][:, :],
                  reads=(rstg[p],), writes=(rwb[p],))
            for ii in range(I // 128):
                pp = ti % 2
                em.op("tensor", "transpose", ps[pp][:, 0:128], wb[p][:, ii * 128:(ii + 1) * 128],
                      self.ident[:, :], reads=(rwb[p], R("ident")), writes=(rps[pp],))
                em.op("scalar", "copy", dst[:, ii * O + oo * 128: ii * O + (oo + 1) * 128],
                      ps[pp][:, 0:128], reads=(rps[pp],), writes=(dst_res,))
                ti += 1

    def emit_rope(self, src, src_res, Aap, Ares, Bap, Bres, xs, xs_res, t1, t1_res,
                  dst_ap, dst_res, W):
        em = self.em
        for b in range(4):
            s, d = 32 * (b ^ 1), 32 * b
            em.op("vector", "tensor_copy", xs[d:d + 32, 0:W], src[s:s + 32, 0:W],
                  reads=(src_res,), writes=(xs_res,))
        em.op("vector", "tensor_tensor", t1[:, 0:W], src[:, 0:W], Aap, ALU.mult,
              reads=(src_res, Ares), writes=(t1_res,))
        em.op("vector", "tensor_tensor", xs[:, 0:W], xs[:, 0:W], Bap, ALU.mult,
              reads=(xs_res, Bres), writes=(xs_res,))
        em.op("vector", "tensor_add", dst_ap, t1[:, 0:W], xs[:, 0:W],
              reads=(t1_res, xs_res), writes=(dst_res,))

    def emit_xn_fm(self, tt0, ntiles, xnt, rxnt, dst, rdst, W, psA, rpsA, psB, rpsB):
        em = self.em
        for j in range(ntiles):
            self.emit_xn_tile(tt0 + j, xnt, rxnt)
            for c in range(DC):
                pr, ps = (rpsA, psA) if (j * DC + c) % 2 == 0 else (rpsB, psB)
                em.op("tensor", "transpose", ps[:, 0:128], xnt[:, c * 128:(c + 1) * 128],
                      self.identf[:, :], reads=(rxnt, self.R("ident")), writes=(pr,))
                em.op("scalar", "copy", dst[:, c * W + j * 128: c * W + (j + 1) * 128],
                      ps[:, 0:128], reads=(pr,), writes=(rdst,))

    def phase_l0(self):
        nc, em, R, d = self.nc, self.em, self.R, self.d
        with ExitStack() as s0:
            K_fm = s0.enter_context(nc.sbuf_tensor("K_fm", [128, DC * T], BF16))
            v_ext = s0.enter_context(nc.sbuf_tensor("v_ext", [128, NH * 16 * 65], BF16))
            cT = s0.enter_context(nc.sbuf_tensor("cT", [128, DC * HALF], BF16))
            carry = s0.enter_context(nc.sbuf_tensor("carry", [128, DC], BF16))
            carry_cx = s0.enter_context(nc.sbuf_tensor("carry_cx", [1, 1], F32))
            rK, rV, rcT = R("K_fm"), R("v_ext"), R("cT")
            rcar, rcarcx = R("carry"), R("carry_cx")

            with ExitStack() as sa:
                xnt = sa.enter_context(nc.sbuf_tensor("a_xnt", [128, D], F32))
                xnq = sa.enter_context(nc.sbuf_tensor("a_xnq", [128, DC * QB], BF16))
                rxnt, rxnq = R("a_xnt"), R("a_xnq")
                psK = [sa.enter_context(nc.psum_tensor(f"a_psK{k}", [128, 512], F32)) for k in range(2)]
                psT = [sa.enter_context(nc.psum_tensor(f"a_psT{k}", [128, 512], F32)) for k in range(2)]
                rpsK = [R("a_psK", k) for k in range(2)]
                rpsT = [R("a_psT", k) for k in range(2)]
                with ExitStack() as s1:
                    mpK = self.mp_scope(s1, nbuf=1)
                    tabA = s1.enter_context(nc.sbuf_tensor("tabA", [128, QB], F32))
                    tabB = s1.enter_context(nc.sbuf_tensor("tabB", [128, QB], F32))
                    wkT = s1.enter_context(nc.sbuf_tensor("wkT", [128, DC * 1024], BF16))
                    kraw = s1.enter_context(nc.sbuf_tensor("a_kraw", [128, QB], F32))
                    xs = s1.enter_context(nc.sbuf_tensor("a_xs", [128, QB], F32))
                    t1 = s1.enter_context(nc.sbuf_tensor("a_t1", [128, QB], F32))
                    bkc = s1.enter_context(nc.sbuf_tensor("a_bkc", [128, DC], F32))
                    rtab, rwkT, rkraw = R("tabs"), R("wkT"), R("a_kraw")
                    rxs, rt1, rbkc = R("a_xs"), R("a_t1"), R("a_bkc")
                    self.dma_in(bkc[:, :], d["bk_p"].rearrange("(c p) -> p c", p=128),
                                writes=(rbkc,), slow=True)
                    self._load_wT_sbuf("wk", d["Wk_p"], wkT, rwkT, D, D, s1)
                    for qb in range(NQB):
                        q0 = qb * QB
                        self.dma_in(tabA[:, :], d["c_ropeA"][:, q0:q0 + QB], writes=(rtab,))
                        self.dma_in(tabB[:, :], d["c_ropeB"][:, q0:q0 + QB], writes=(rtab,))
                        self.emit_xn_fm(qb * 4, 4, xnt, rxnt, xnq, rxnq, QB,
                                        psT[0], rpsT[0], psT[1], rpsT[1])
                        for mc in range(DC):
                            pr, ps = rpsK[mc % 2], psK[mc % 2]
                            for ii in range(DC):
                                self.mm(pr, ps[:, :],
                                        wkT[:, ii * 1024 + mc * 128: ii * 1024 + (mc + 1) * 128],
                                        xnq[:, ii * QB:(ii + 1) * QB],
                                        (rwkT, rxnq), ii == 0, ii == DC - 1)
                            em.op("vector", "tensor_scalar", kraw[:, :], ps[:, :],
                                  bkc[:, mc:mc + 1], None, ALU.add,
                                  reads=(pr, rbkc), writes=(rkraw,))
                            self.emit_rope(kraw, rkraw, tabA[:, :], rtab,
                                           tabB[:, :], rtab, xs, rxs, t1, rt1,
                                           K_fm[:, mc * T + q0: mc * T + q0 + QB], rK, QB)
                            self.mp_step(mpK, 2)
                    self.mp_flush(mpK)
                    self.barrier()
                with ExitStack() as s2:
                    mpV = self.mp_scope(s2, nbuf=1)
                    wvT = s2.enter_context(nc.sbuf_tensor("wvT", [128, DC * 1024], BF16))
                    bvr = s2.enter_context(nc.sbuf_tensor("a_bvr", [1, D], F32))
                    bvb = s2.enter_context(nc.sbuf_tensor("a_bvb", [1, D], BF16))
                    rwvT, rbvb = R("wvT"), R("a_bvb")
                    self.dma_in(bvr[:, :], d["L0_attn_bv"].rearrange("(o d) -> o d", o=1), writes=(R("a_bvr"),))
                    em.op("vector", "tensor_copy", bvb[:, :], bvr[:, :],
                          reads=(R("a_bvr"),), writes=(rbvb,))
                    em.op("vector", "memset", v_ext[:, :], 1.0, writes=(rV,))
                    self._load_wT_sbuf("wv", d["L0_attn_Wv"], wvT, rwvT, D, D, s2)
                    for qb in range(NQB):
                        self.emit_xn_fm(qb * 4, 4, xnt, rxnt, xnq, rxnq, QB,
                                        psT[0], rpsT[0], psT[1], rpsT[1])
                        for j in range(4):
                            for bk in range(2):
                                pr, ps = rpsK[bk], psK[bk]
                                for ii in range(DC):
                                    self.mm(pr, ps[:, :],
                                            xnq[:, ii * QB + j * 128: ii * QB + (j + 1) * 128],
                                            wvT[:, ii * 1024 + bk * 512: ii * 1024 + (bk + 1) * 512],
                                            (rxnq, rwvT), ii == 0, False)
                                self.mm(pr, ps[:, :], self.ones_row_bf[0:1, :],
                                        bvb[0:1, bk * 512:(bk + 1) * 512],
                                        (R("ones"), rbvb), False, True)
                                kc = qb * 4 + j
                                for hh in range(8):
                                    h = bk * 8 + hh
                                    em.op("vector", "tensor_copy",
                                          v_ext[:, (h * 16 + kc) * 65:(h * 16 + kc) * 65 + 64],
                                          ps[:, hh * 64:(hh + 1) * 64],
                                          reads=(pr,), writes=(rV,))
                                self.mp_step(mpV, 2)
                    self.mp_flush(mpV)
                    self.barrier()

                self.barrier()

            self.tap("K_fm", K_fm)
            self.tap("v_ext", v_ext)
            for half in range(2):
                self.phase_B0(half, K_fm, rK, v_ext, rV, cT, rcT)
                self.phase_C0(half, cT, rcT, carry, rcar, carry_cx, rcarcx)
        self.tap("x_l0", self.x)

    def phase_B0(self, half, K_fm, rK, v_ext, rV, cT, rcT):
        nc, em, R, d = self.nc, self.em, self.R, self.d
        with ExitStack() as sb:
            mpB = self.mp_scope(sb, nbuf=1)
            wqT = sb.enter_context(nc.sbuf_tensor("wqT", [128, DC * 1024], BF16))
            xnt = sb.enter_context(nc.sbuf_tensor("b_xnt", [128, D], F32))
            xnq = sb.enter_context(nc.sbuf_tensor("b_xnq", [128, DC * QB], BF16))
            Qq = sb.enter_context(nc.sbuf_tensor("b_Qq", [128, DC * QB], BF16))
            qraw = sb.enter_context(nc.sbuf_tensor("b_qraw", [128, QB], F32))
            xs = sb.enter_context(nc.sbuf_tensor("b_xs", [128, QB], F32))
            t1 = sb.enter_context(nc.sbuf_tensor("b_t1", [128, QB], F32))
            tabA = sb.enter_context(nc.sbuf_tensor("b_tabA", [128, QB], F32))
            tabB = sb.enter_context(nc.sbuf_tensor("b_tabB", [128, QB], F32))
            bqc = sb.enter_context(nc.sbuf_tensor("b_bqc", [128, DC], F32))
            pT = [sb.enter_context(nc.sbuf_tensor(f"b_pT{k}", [128, QB], BF16)) for k in range(2)]
            zT = sb.enter_context(nc.sbuf_tensor("b_zT", [128, 8], F32))
            rcp = sb.enter_context(nc.sbuf_tensor("b_rcp", [128, 8], F32))
            zsb = qraw
            rzs = xs
            rzb = t1
            psS = [sb.enter_context(nc.psum_tensor(f"b_psS{k}", [128, 512], F32)) for k in range(2)]
            psO = [sb.enter_context(nc.psum_tensor(f"b_psO{k}", [128, 512], F32)) for k in range(2)]
            psB = sb.enter_context(nc.psum_tensor("b_psB", [128, 512], F32))
            rwqT, rxnt, rxnq, rQq, rqraw = R("wqT"), R("b_xnt"), R("b_xnq"), R("b_Qq"), R("b_qraw")
            rxs, rt1, rtab, rbqc = R("b_xs"), R("b_t1"), R("b_tab"), R("b_bqc")
            rpT = [R("b_pT", k) for k in range(2)]
            rzsb, rzT, rrcp, rrzs, rrzb = R("b_qraw"), R("b_zT"), R("b_rcp"), R("b_xs"), R("b_t1")
            rpsS = [R("b_psS", k) for k in range(2)]
            rpsO = [R("b_psO", k) for k in range(2)]
            rpsB = R("b_psB")

            self.dma_in(bqc[:, :], d["bq_p"].rearrange("(c p) -> p c", p=128),
                        writes=(rbqc,), slow=True)
            em.op("vector", "tensor_scalar", bqc[:, :], bqc[:, :], 0.125, None, ALU.mult,
                  reads=(rbqc,), writes=(rbqc,))
            self._load_wT_sbuf("wq", d["Wq_p"], wqT, rwqT, D, D, sb,
                               stg_ext=(mpB["stg"][0], mpB["rstg"][0]))
            for ql in range(2):
                qb = half * 2 + ql
                q0 = qb * QB
                self.dma_in(tabA[:, :], d["c_ropeA"][:, q0:q0 + QB], writes=(rtab,))
                self.dma_in(tabB[:, :], d["c_ropeB"][:, q0:q0 + QB], writes=(rtab,))
                self.emit_xn_fm(qb * 4, 4, xnt, rxnt, xnq, rxnq, QB,
                                psS[0], rpsS[0], psS[1], rpsS[1])
                for mc in range(DC):
                    pr, ps = rpsS[mc % 2], psS[mc % 2]
                    for ii in range(DC):
                        self.mm(pr, ps[:, :],
                                wqT[:, ii * 1024 + mc * 128: ii * 1024 + (mc + 1) * 128],
                                xnq[:, ii * QB:(ii + 1) * QB],
                                (rwqT, rxnq), ii == 0, ii == DC - 1)
                    em.op("vector", "tensor_scalar", qraw[:, :], ps[:, :],
                          0.125, bqc[:, mc:mc + 1], ALU.mult, ALU.add,
                          reads=(pr, rbqc), writes=(rqraw,))
                    self.emit_rope(qraw, rqraw, tabA[:, :], rtab, tabB[:, :], rtab,
                                   xs, rxs, t1, rt1,
                                   Qq[:, mc * QB:(mc + 1) * QB], rQq, QB)
                nkc = qb * 4 + 4
                for h in range(NH):
                    hb = 64 * (h % 2)
                    po = h % 2
                    for kc in range(nkc):
                        ss = kc % 2
                        self.mm(rpsS[ss], psS[ss][:, :],
                                K_fm[hb:hb + 64, (h // 2) * T + kc * 128:(h // 2) * T + (kc + 1) * 128],
                                Qq[hb:hb + 64, (h // 2) * QB:(h // 2 + 1) * QB],
                                (rK, rQq), True, True)
                        em.op("scalar", "activation", pT[ss][:, :], psS[ss][:, :], ACT.Exp,
                              reads=(rpsS[ss],), writes=(rpT[ss],))
                        j = kc - qb * 4
                        if j >= 0:
                            em.op("vector", "tensor_mul", pT[ss][:, :], pT[ss][:, :],
                                  self.masks[:, j * 512:(j + 1) * 512],
                                  reads=(rpT[ss], R("masks")), writes=(rpT[ss],))
                        self.mm(rpsO[po], psO[po][0:65, 0:512],
                                v_ext[:, (h * 16 + kc) * 65:(h * 16 + kc + 1) * 65],
                                pT[ss][:, :], (rV, rpT[ss]), kc == 0, kc == nkc - 1)
                    if h % 2 == 1:
                        for i in range(2):
                            em.op("vector", "tensor_copy", zsb[32 * i:32 * i + 1, 0:512],
                                  psO[i][64:65, 0:512], reads=(rpsO[i],), writes=(rzsb,))
                        for i in range(2):
                            for s in range(4):
                                em.op("tensor", "transpose", psB[:, 0:1],
                                      zsb[32 * i:32 * i + 1, s * 128:(s + 1) * 128],
                                      self.identf[32 * i:32 * i + 1, 32 * i:32 * i + 1],
                                      reads=(rzsb, R("ident")), writes=(rpsB,))
                                em.op("vector", "tensor_copy", zT[:, i * 4 + s:i * 4 + s + 1],
                                      psB[:, 0:1], reads=(rpsB,), writes=(rzT,))
                        em.op("vector", "reciprocal", rcp[:, :], zT[:, :],
                              reads=(rzT,), writes=(rrcp,))
                        for i in range(2):
                            for s in range(4):
                                em.op("tensor", "transpose", psB[0:1, 0:128],
                                      rcp[:, i * 4 + s:i * 4 + s + 1], self.identf[:, :],
                                      reads=(rrcp, R("ident")), writes=(rpsB,))
                                em.op("vector", "tensor_copy",
                                      rzs[32 * i:32 * i + 1, s * 128:(s + 1) * 128],
                                      psB[0:1, 0:128], reads=(rpsB,), writes=(rrzs,))
                        for i in range(2):
                            hh = h - 1 + i
                            self.mm(rpsB, psB[:, :], self.ones_row_f[32 * i:32 * i + 1, :],
                                    rzs[32 * i:32 * i + 1, 0:512], (R("ones"), rrzs), True, True)
                            em.op("vector", "tensor_copy", rzb[:, :], psB[:, :],
                                  reads=(rpsB,), writes=(rrzb,))
                            em.op("vector", "tensor_mul",
                                  cT[64 * (hh % 2):64 * (hh % 2) + 64,
                                     (hh // 2) * HALF + ql * QB:(hh // 2) * HALF + (ql + 1) * QB],
                                  psO[i][0:64, 0:512], rzb[0:64, :],
                                  reads=(rpsO[i], rrzb), writes=(rcT,))
                    self.mp_step(mpB, 4)
            self.mp_flush(mpB)
            self.barrier()
        self.tap(f"cT{half}", cT)

    def phase_C0(self, half, cT, rcT, carry, rcar, carry_cx, rcarcx):
        nc, em, R, d = self.nc, self.em, self.R, self.d
        h0 = half * HALF
        with ExitStack() as sc:
            xqf = sc.enter_context(nc.sbuf_tensor("c_xqf", [128, DC * 1026], BF16))
            cxr = sc.enter_context(nc.sbuf_tensor("c_cxr", [1, 1026], F32))
            rxqf, rcxr = R("c_xqf"), R("c_cxr")
            with ExitStack() as sp:
                xnt = sp.enter_context(nc.sbuf_tensor("c_xnt", [128, D], F32))
                vb = sp.enter_context(nc.sbuf_tensor("c_vb", [128, D], F32))
                xqt = sp.enter_context(nc.sbuf_tensor("c_xqt", [128, D], BF16))
                scol = sp.enter_context(nc.sbuf_tensor("c_scol", [128, 2], F32))
                psT = [sp.enter_context(nc.psum_tensor(f"c_psT{k}", [128, 1024], BF16)) for k in range(2)]
                psF = sp.enter_context(nc.psum_tensor("c_psF", [128, 512], F32))
                rxnt, rvb, rxqt, rscol = R("c_xnt"), R("c_vb"), R("c_xqt"), R("c_scol")
                rpsT = [R("c_psT", k) for k in range(2)]
                rpsF = R("c_psF")
                ntl = 9 if half == 0 else 8
                for tl in range(ntl):
                    tt = half * 8 + tl
                    edge = (tl == 8)
                    self.emit_xn_tile(tt, xnt, rxnt)
                    em.op("vector", "reduce_max", scol[:, 0:1], xnt[:, :], AX.X,
                          apply_absolute_value=True, reads=(rxnt,), writes=(rscol,))
                    em.op("vector", "tensor_scalar", scol[:, 0:1], scol[:, 0:1], 1e-5,
                          None, ALU.max, reads=(rscol,), writes=(rscol,))
                    em.op("vector", "reciprocal", scol[:, 1:2], scol[:, 0:1],
                          reads=(rscol,), writes=(rscol,))
                    em.op("vector", "tensor_scalar", scol[:, 1:2], scol[:, 1:2], 127.0,
                          None, ALU.mult, reads=(rscol,), writes=(rscol,))
                    em.op("vector", "tensor_scalar", vb[:, :], xnt[:, :], scol[:, 1:2],
                          None, ALU.mult, reads=(rxnt, rscol), writes=(rvb,))
                    em.op("vector", "tensor_scalar", xqt[:, :], vb[:, :], 12582912.0,
                          12582912.0, ALU.add, ALU.subtract, reads=(rvb,), writes=(rxqt,))
                    em.op("vector", "reciprocal", self.cxcols[:, tt % NTT:tt % NTT + 1],
                          scol[:, 1:2], reads=(rscol,), writes=(R("cxcols"),))
                    for c in range(DC):
                        pp = c % 2
                        em.op("tensor", "transpose", psT[pp][:, 0:128],
                              xqt[:, c * 128:(c + 1) * 128], self.ident[:, :],
                              reads=(rxqt, R("ident")), writes=(rpsT[pp],))
                        if edge:
                            em.op("scalar", "copy", xqf[:, c * 1026 + 1025: c * 1026 + 1026],
                                  psT[pp][:, 0:1], reads=(rpsT[pp],), writes=(rxqf,))
                        else:
                            em.op("scalar", "copy",
                                  xqf[:, c * 1026 + 1 + tl * 128: c * 1026 + 1 + (tl + 1) * 128],
                                  psT[pp][:, 0:128], reads=(rpsT[pp],), writes=(rxqf,))
                    em.op("tensor", "transpose", psF[0:1, 0:128],
                          self.cxcols[:, tt % NTT:tt % NTT + 1], self.identf[:, :],
                          reads=(R("cxcols"), R("ident")), writes=(rpsF,))
                    if edge:
                        em.op("vector", "tensor_copy", cxr[0:1, 1025:1026],
                              psF[0:1, 0:1], reads=(rpsF,), writes=(rcxr,))
                    else:
                        em.op("vector", "tensor_copy", cxr[0:1, 1 + tl * 128: 1 + (tl + 1) * 128],
                              psF[0:1, 0:128], reads=(rpsF,), writes=(rcxr,))
                if half == 0:
                    em.op("vector", "memset",
                          xqf[:, :].rearrange("p (c w) -> p c w", w=1026)[:, :, 0:1], 0.0,
                          writes=(rxqf,))
                    em.op("vector", "memset", cxr[0:1, 0:1], 0.0, writes=(rcxr,))
                    em.op("vector", "tensor_copy", carry[:, :],
                          xqf[:, :].rearrange("p (c w) -> p c w", w=1026)[:, :, 1024:1025].rearrange("p c w -> p (c w)"),
                          reads=(rxqf,), writes=(rcar,))
                    em.op("vector", "tensor_copy", carry_cx[0:1, :], cxr[0:1, 1024:1025],
                          reads=(rcxr,), writes=(rcarcx,))
                else:
                    em.op("vector", "tensor_copy",
                          xqf[:, :].rearrange("p (c w) -> p c w", w=1026)[:, :, 0:1].rearrange("p c w -> p (c w)"),
                          carry[:, :], reads=(rcar,), writes=(rxqf,))
                    em.op("vector", "tensor_copy", cxr[0:1, 0:1], carry_cx[0:1, :],
                          reads=(rcarcx,), writes=(rcxr,))
                    em.op("vector", "memset",
                          xqf[:, :].rearrange("p (c w) -> p c w", w=1026)[:, :, 1025:1026], 0.0,
                          writes=(rxqf,))
                    em.op("vector", "memset", cxr[0:1, 1025:1026], 0.0, writes=(rcxr,))
                self.barrier()
            with ExitStack() as sm:
                w1m = [sm.enter_context(nc.sbuf_tensor(f"c_w1m{k}", [128, DC * 128], BF16)) for k in range(2)]
                w2m = [sm.enter_context(nc.sbuf_tensor(f"c_w2m{k}", [128, 1024], BF16)) for k in range(2)]
                wot = [sm.enter_context(nc.sbuf_tensor(f"c_wot{k}", [128, 1024], BF16)) for k in range(2)]
                cwc = sm.enter_context(nc.sbuf_tensor("c_cwc", [128, HC * 3], F32))
                cbc = sm.enter_context(nc.sbuf_tensor("c_cbc", [128, HC], F32))
                bob = sm.enter_context(nc.sbuf_tensor("c_bob", [1, D], BF16))
                cxb = sm.enter_context(nc.sbuf_tensor("c_cxb", [128, 258], F32))
                hx = sm.enter_context(nc.sbuf_tensor("c_hx", [128, 258], F32))
                cva = sm.enter_context(nc.sbuf_tensor("c_cva", [128, 512], F32))
                h2 = sm.enter_context(nc.sbuf_tensor("c_h2", [128, HC * 256], BF16))
                am = sm.enter_context(nc.sbuf_tensor("c_am", [128, 256], F32))
                s127b = sm.enter_context(nc.sbuf_tensor("c_s127b", [128, 256], F32))
                c2c = sm.enter_context(nc.sbuf_tensor("c_c2c", [128, 2], F32))
                amx = sm.enter_context(nc.sbuf_tensor("c_amx", [128, 4], F32))
                xq2 = [sm.enter_context(nc.sbuf_tensor(f"c_xq2{k}", [128, 128], BF16)) for k in range(2)]
                gq = cva
                psH = [sm.enter_context(nc.psum_tensor(f"c_psH{k}", [128, 512], F32)) for k in range(2)]
                psY = [sm.enter_context(nc.psum_tensor(f"c_psY{k}", [128, 512], F32)) for k in range(4)]
                psSm = [sm.enter_context(nc.psum_tensor(f"c_psS{k}", [128, 512], F32)) for k in range(2)]
                rw1m = [R("c_w1m", k) for k in range(2)]
                rw2m = [R("c_w2m", k) for k in range(2)]
                rwot = [R("c_wot", k) for k in range(2)]
                rcw, rcb, rbob = R("c_cwc"), R("c_cbc"), R("c_bob")
                rcxb, rhx, rcva, rh2 = R("c_cxb"), R("c_hx"), R("c_cva"), R("c_h2")
                ram, rs127b, rc2c = R("c_am"), R("c_s127b"), R("c_c2c")
                ramx = R("c_amx")
                rxq2 = [R("c_xq2", k) for k in range(2)]
                rgq = R("c_cva")
                rpsH = [R("c_psH", k) for k in range(2)]
                rpsY = [R("c_psY", k) for k in range(4)]
                rpsSm = [R("c_psS", k) for k in range(2)]

                self.dma_in(cwc[:, :].rearrange("p (c k) -> p c k", k=3),
                            d["L0_ffn_conv_w"].rearrange("(c p) one k -> p c (one k)", p=128),
                            writes=(rcw,), slow=True)
                self.dma_in(cbc[:, :], d["L0_ffn_conv_b"].rearrange("(c p) -> p c", p=128),
                            writes=(rcb,), slow=True)
                for c2 in range(2):
                    self.dma_in(gq[0:1, 0:512],
                                d["L0_attn_bo"].rearrange("(o d) -> o d", o=1)[:, c2 * 512:(c2 + 1) * 512],
                                writes=(rgq,))
                    em.op("vector", "tensor_copy", bob[0:1, c2 * 512:(c2 + 1) * 512],
                          gq[0:1, 0:512], reads=(rgq,), writes=(rbob,))
                for grp in range(4):
                    tl0 = grp * 256
                    self.mm(rpsH[0], psH[0][:, 0:258], self.ones_row_f[0:1, :],
                            cxr[0:1, tl0:tl0 + 258], (R("ones"), rcxr), True, True)
                    em.op("vector", "tensor_copy", cxb[:, :], psH[0][:, 0:258],
                          reads=(rpsH[0],), writes=(rcxb,))
                    for m in range(HC):
                        p = m % 2
                        self.dma_in(w1m[p][:, :],
                                    d["w1qT_d"][:, m * DC * 128:(m + 1) * DC * 128],
                                    writes=(rw1m[p],), chan=self.cW[p],
                                    extra_waits=tuple(R("w1", "dram").wt.items()))
                        ph = m % 2
                        for dd in range(DC):
                            self.mm(rpsH[ph], psH[ph][:, 0:258],
                                    w1m[p][:, dd * 128:(dd + 1) * 128],
                                    xqf[:, dd * 1026 + tl0: dd * 1026 + tl0 + 258],
                                    (rw1m[p], rxqf), dd == 0, dd == DC - 1)
                        em.op("vector", "scalar_tensor_tensor", hx[:, :], psH[ph][:, 0:258],
                              self.gcols[:, 1:2], cxb[:, :], ALU.mult, ALU.mult,
                              reads=(rpsH[ph], R("gcols"), rcxb), writes=(rhx,))
                        em.op("vector", "tensor_scalar", cva[:, 0:256], hx[:, 0:256],
                              cwc[:, m * 3:m * 3 + 1], cbc[:, m:m + 1], ALU.mult, ALU.add,
                              reads=(rhx, rcw, rcb), writes=(rcva,))
                        em.op("vector", "scalar_tensor_tensor", cva[:, 0:256], hx[:, 1:257],
                              cwc[:, m * 3 + 1:m * 3 + 2], cva[:, 0:256], ALU.mult, ALU.add,
                              reads=(rhx, rcw, rcva), writes=(rcva,))
                        em.op("vector", "scalar_tensor_tensor", cva[:, 0:256], hx[:, 2:258],
                              cwc[:, m * 3 + 2:m * 3 + 3], cva[:, 0:256], ALU.mult, ALU.add,
                              reads=(rhx, rcw, rcva), writes=(rcva,))
                        em.op("scalar", "activation", h2[:, m * 256:(m + 1) * 256],
                              cva[:, 0:256], ACT.Silu, reads=(rcva,), writes=(rh2,))
                        if m == 0:
                            em.op("vector", "scalar_tensor_tensor", am[:, :],
                                  h2[:, 0:256], -1.0, h2[:, 0:256], ALU.mult, ALU.max,
                                  reads=(rh2,), writes=(ram,))
                        else:
                            em.op("vector", "scalar_tensor_tensor", am[:, :],
                                  h2[:, m * 256:(m + 1) * 256], -1.0, am[:, :],
                                  ALU.mult, ALU.max, reads=(rh2, ram), writes=(ram,))
                            em.op("vector", "tensor_tensor", am[:, :], am[:, :],
                                  h2[:, m * 256:(m + 1) * 256], ALU.max,
                                  reads=(ram, rh2), writes=(ram,))
                    for s2 in range(2):
                        em.op("tensor", "transpose", psH[s2][:, 0:128],
                              am[:, s2 * 128:(s2 + 1) * 128],
                              self.identf[:, :], reads=(ram, R("ident")), writes=(rpsH[s2],))
                        em.op("vector", "tensor_copy", cva[:, s2 * 128:(s2 + 1) * 128],
                              psH[s2][:, 0:128], reads=(rpsH[s2],), writes=(rcva,))
                        em.op("vector", "reduce_max", amx[:, s2:s2 + 1],
                              cva[:, s2 * 128:(s2 + 1) * 128], AX.X,
                              reads=(rcva,), writes=(ramx,))
                    em.op("vector", "tensor_scalar", amx[:, 0:2], amx[:, 0:2], 1e-5,
                          None, ALU.max, reads=(ramx,), writes=(ramx,))
                    em.op("vector", "tensor_scalar", c2c[:, 0:2], amx[:, 0:2],
                          self.gcols[:, 3:4], 1.0 / 127.0, ALU.mult, ALU.mult,
                          reads=(ramx, R("gcols")), writes=(rc2c,))
                    em.op("vector", "reciprocal", amx[:, 2:4], amx[:, 0:2],
                          reads=(ramx,), writes=(ramx,))
                    em.op("vector", "tensor_scalar", amx[:, 2:4], amx[:, 2:4], 127.0,
                          None, ALU.mult, reads=(ramx,), writes=(ramx,))
                    for s2 in range(2):
                        em.op("tensor", "transpose", psH[s2][0:1, 128:256],
                              amx[:, 2 + s2:3 + s2],
                              self.identf[:, :], reads=(ramx, R("ident")), writes=(rpsH[s2],))
                        em.op("vector", "tensor_copy", s127b[0:1, s2 * 128:(s2 + 1) * 128],
                              psH[s2][0:1, 128:256], reads=(rpsH[s2],), writes=(rs127b,))
                    self.mm(rpsH[0], psH[0][:, 256:512], self.ones_row_f[0:1, :],
                            s127b[0:1, 0:256], (R("ones"), rs127b), True, True)
                    em.op("vector", "tensor_copy", s127b[:, :], psH[0][:, 256:512],
                          reads=(rpsH[0],), writes=(rs127b,))
                    for m in range(HC):
                        p = m % 2
                        self.dma_in(w2m[p][:, :], d["w2qT_d"][m * 128:(m + 1) * 128, :],
                                    writes=(rw2m[p],), chan=self.cW[2 + p],
                                    extra_waits=tuple(R("w2", "dram").wt.items()))
                        for fq01 in range(2):
                            q = fq01
                            em.op("vector", "tensor_tensor", hx[:, 0:128],
                                  h2[:, m * 256 + fq01 * 128: m * 256 + (fq01 + 1) * 128],
                                  s127b[:, fq01 * 128:(fq01 + 1) * 128], ALU.mult,
                                  reads=(rh2, rs127b), writes=(rhx,))
                            em.op("vector", "tensor_scalar", xq2[q][:, :], hx[:, 0:128],
                                  12582912.0, 12582912.0, ALU.add, ALU.subtract,
                                  reads=(rhx,), writes=(rxq2[q],))
                            for dh in range(2):
                                self.mm(rpsY[fq01 * 2 + dh], psY[fq01 * 2 + dh][:, :],
                                        xq2[q][:, :],
                                        w2m[p][:, dh * 512:(dh + 1) * 512],
                                        (rxq2[q], rw2m[p]), m == 0, m == HC - 1)
                    for fq01 in range(2):
                        tt = half * 8 + grp * 2 + fq01
                        for vf in range(DC):
                            p = vf % 2
                            self.dma_in(wot[p][:, :], d["WoT_d"][vf * 128:(vf + 1) * 128, :],
                                        writes=(rwot[p],), chan=self.cW[4 + p],
                                        extra_waits=tuple(R("wo", "dram").wt.items()))
                            for dh in range(2):
                                self.mm(rpsSm[dh], psSm[dh][:, :],
                                        cT[:, vf * HALF + tl0 + fq01 * 128: vf * HALF + tl0 + fq01 * 128 + 128],
                                        wot[p][:, dh * 512:(dh + 1) * 512],
                                        (rcT, rwot[p]), vf == 0, False)
                        for dh in range(2):
                            self.mm(rpsSm[dh], psSm[dh][:, :], self.ones_row_bf[0:1, :],
                                    bob[0:1, dh * 512:(dh + 1) * 512],
                                    (R("ones"), rbob), False, True)
                        for dh in range(2):
                            em.op("scalar", "activation", gq[:, 0:512],
                                  psY[fq01 * 2 + dh][:, :], ACT.Silu, scale=c2c[:, fq01:fq01 + 1],
                                  reads=(rpsY[fq01 * 2 + dh], rc2c), writes=(rgq,))
                            em.op("vector", "tensor_mul", gq[:, 0:512],
                                  psSm[dh][:, :], gq[:, 0:512],
                                  reads=(rpsSm[dh], rgq), writes=(rgq,))
                            em.op("vector", "tensor_add",
                                  self.x[:, tt * D + dh * 512: tt * D + (dh + 1) * 512],
                                  self.x[:, tt * D + dh * 512: tt * D + (dh + 1) * 512],
                                  gq[:, 0:512], reads=(rgq, R("x")), writes=(R("x"),))
                self.barrier()

    def _xn_fm_range(self, tt0, ntiles, xnt, rxnt, dst, rdst, W, psA, rpsA, psB, rpsB):
        self.emit_xn_fm(tt0, ntiles, xnt, rxnt, dst, rdst, W, psA, rpsA, psB, rpsB)

    def phase_l1(self):
        nc, em, R, d = self.nc, self.em, self.R, self.d
        with ExitStack() as s1:
            acc = s1.enter_context(nc.sbuf_tensor("acc", [128, NTT * D], F32))
            we = s1.enter_context(nc.sbuf_tensor("we", [128, NTT * NE], F32))
            racc, rwe = R("acc"), R("we")

            with ExitStack() as sr:
                xn2 = sr.enter_context(nc.sbuf_tensor("r_xn2", [128, DC * HALF], BF16))
                wef = sr.enter_context(nc.sbuf_tensor("r_wef", [8, T], BF16))
                stg = [sr.enter_context(nc.sbuf_tensor(f"r_stg{k}", [128, DC * 128], F32)) for k in range(2)]
                w1b = [sr.enter_context(nc.sbuf_tensor(f"r_w1b{k}", [128, DC * 128], BF16)) for k in range(2)]
                rw2s = sr.enter_context(nc.sbuf_tensor("r_rw2s", [128, HC * NE], F32))
                rw2b = sr.enter_context(nc.sbuf_tensor("r_rw2b", [128, HC * NE], BF16))
                rb1c = sr.enter_context(nc.sbuf_tensor("r_rb1c", [128, HC], F32))
                rb2c = sr.enter_context(nc.sbuf_tensor("r_rb2c", [8, 1], F32))
                tb = [sr.enter_context(nc.sbuf_tensor(f"r_tb{k}", [128, QB], BF16)) for k in range(2)]
                lg = sr.enter_context(nc.sbuf_tensor("r_lg", [8, QB], F32))
                lgt = sr.enter_context(nc.sbuf_tensor("r_lgt", [128, NE], F32))
                l2v = sr.enter_context(nc.sbuf_tensor("r_l2v", [128, NE], F32))
                eq1 = sr.enter_context(nc.sbuf_tensor("r_eq1", [128, NE], F32))
                mm1c = sr.enter_context(nc.sbuf_tensor("r_m1", [128, 4], F32))
                b2b = sr.enter_context(nc.sbuf_tensor("r_b2b", [8, D], BF16))
                b2s = sr.enter_context(nc.sbuf_tensor("r_b2s", [8, D], F32))
                xnt = sr.enter_context(nc.sbuf_tensor("r_xnt", [128, D], F32))
                psT = [sr.enter_context(nc.psum_tensor(f"r_psT{k}", [128, 512], F32)) for k in range(2)]
                psL = [sr.enter_context(nc.psum_tensor(f"r_psL{k}", [128, 512], F32)) for k in range(2)]
                rxn2, rwef = R("r_xn2"), R("r_wef")
                rstg = [R("r_stg", k) for k in range(2)]
                rw1bb = [R("r_w1b", k) for k in range(2)]
                rrw2b, rrb1, rrb2 = R("r_rw2b"), R("r_rb1c"), R("r_rb2c")
                rtb = [R("r_tb", k) for k in range(2)]
                rlg, rlgt, rl2v, req1, rm1 = (R("r_lg"), R("r_lgt"),
                                              R("r_l2v"), R("r_eq1"), R("r_m1"))
                rb2b, rxnt = R("r_b2b"), R("r_xnt")
                rpsT = [R("r_psT", k) for k in range(2)]
                rpsL = [R("r_psL", k) for k in range(2)]

                mpR = self.mp_scope(sr)
                while self._mp_i < len(self._mp):
                    self.mp_step(mpR, 8)
                self.mp_flush(mpR)
                self.dma_in(rw2s[:, :].rearrange("p (c e) -> p c e", e=NE),
                            d["L1_ffn_rw2"].rearrange("(c p) e -> p c e", p=128),
                            writes=(R("r_rw2s"),), slow=True)
                em.op("vector", "tensor_copy", rw2b[:, :], rw2s[:, :],
                      reads=(R("r_rw2s"),), writes=(rrw2b,))
                self.dma_in(rb1c[:, :], d["L1_ffn_rb1"].rearrange("(c p) -> p c", p=128),
                            writes=(rrb1,), slow=True)
                self.dma_in(rb2c[:, :], d["L1_ffn_rb2"].rearrange("(e o) -> e o", o=1),
                            writes=(rrb2,), slow=True)
                self.dma_in(b2s[:, :], d["L1_ffn_b2"][:, :], writes=(R("r_b2s"),))
                em.op("vector", "tensor_copy", b2b[:, :], b2s[:, :],
                      reads=(R("r_b2s"),), writes=(rb2b,))
                for half in range(2):
                    self.emit_xn_fm(half * 8, 8, xnt, rxnt, xn2, rxn2, HALF,
                                    psT[0], rpsT[0], psT[1], rpsT[1])
                    for hc in range(HC):
                        p = hc % 2
                        self.dma_in(stg[p][:, :], d["rw1_p"][:, hc * DC * 128:(hc + 1) * DC * 128],
                                    writes=(rstg[p],), chan=self.cW[p])
                        em.op("vector", "tensor_copy", w1b[p][:, :], stg[p][:, :],
                              reads=(rstg[p],), writes=(rw1bb[p],))
                        for ql in range(2):
                            for dd in range(DC):
                                self.mm(rpsT[ql], psT[ql][:, :],
                                        w1b[p][:, dd * 128:(dd + 1) * 128],
                                        xn2[:, dd * HALF + ql * 512: dd * HALF + (ql + 1) * 512],
                                        (rw1bb[p], rxn2), dd == 0, dd == DC - 1)
                            em.op("scalar", "activation", tb[ql][:, :], psT[ql][:, :], ACT.Silu,
                                  bias=rb1c[:, hc:hc + 1], reads=(rpsT[ql], rrb1), writes=(rtb[ql],))
                            self.mm(rpsL[ql], psL[ql][0:8, :], rw2b[:, hc * NE:(hc + 1) * NE],
                                    tb[ql][:, :], (rrw2b, rtb[ql]), hc == 0, hc == HC - 1)
                    for ql in range(2):
                        qb = half * 2 + ql
                        em.op("vector", "tensor_scalar", lg[:, :], psL[ql][0:8, :],
                              rb2c[:, 0:1], None, ALU.add, reads=(rpsL[ql], rrb2), writes=(rlg,))
                        for tl in range(4):
                            tt = qb * 4 + tl
                            em.op("tensor", "transpose", psL[ql][0:128, 8:16],
                                  lg[0:8, tl * 128:(tl + 1) * 128], self.identf[0:8, 0:8],
                                  reads=(rlg, R("ident")), writes=(rpsL[ql],))
                            em.op("vector", "tensor_copy", lgt[:, :], psL[ql][:, 8:16],
                                  reads=(rpsL[ql],), writes=(rlgt,))
                            em.op("vector", "reduce_max", mm1c[:, 0:1], lgt[:, :], AX.X,
                                  reads=(rlgt,), writes=(rm1,))
                            em.op("vector", "tensor_scalar", eq1[:, :], lgt[:, :],
                                  mm1c[:, 0:1], None, ALU.is_equal,
                                  reads=(rlgt, rm1), writes=(req1,))
                            em.op("vector", "scalar_tensor_tensor", l2v[:, :], eq1[:, :],
                                  NEG_TIE, lgt[:, :], ALU.mult, ALU.add,
                                  reads=(req1, rlgt), writes=(rl2v,))
                            em.op("vector", "reduce_max", mm1c[:, 1:2], l2v[:, :], AX.X,
                                  reads=(rl2v,), writes=(rm1,))
                            em.op("vector", "tensor_sub", mm1c[:, 2:3], mm1c[:, 0:1],
                                  mm1c[:, 1:2], reads=(rm1,), writes=(rm1,))
                            em.op("scalar", "activation", mm1c[:, 2:3], mm1c[:, 2:3],
                                  ACT.Sigmoid, reads=(rm1,), writes=(rm1,))
                            em.op("vector", "tensor_scalar", mm1c[:, 3:4], mm1c[:, 2:3],
                                  -1.0, 1.0, ALU.mult, ALU.add, reads=(rm1,), writes=(rm1,))
                            em.op("vector", "tensor_scalar", eq1[:, :], eq1[:, :],
                                  mm1c[:, 2:3], None, ALU.mult, reads=(req1, rm1), writes=(req1,))
                            em.op("vector", "tensor_scalar", l2v[:, :], l2v[:, :],
                                  mm1c[:, 1:2], None, ALU.is_equal,
                                  reads=(rl2v, rm1), writes=(rl2v,))
                            em.op("vector", "scalar_tensor_tensor", we[:, tt * NE:(tt + 1) * NE],
                                  l2v[:, :], mm1c[:, 3:4], eq1[:, :], ALU.mult, ALU.add,
                                  reads=(rl2v, rm1, req1), writes=(rwe,))
                            em.op("tensor", "transpose", psL[ql][0:8, 16:144],
                                  we[:, tt * NE:(tt + 1) * NE], self.identf[:, :],
                                  reads=(rwe, R("ident")), writes=(rpsL[ql],))
                            em.op("vector", "tensor_copy", wef[0:8, tt * 128:(tt + 1) * 128],
                                  psL[ql][0:8, 16:144], reads=(rpsL[ql],), writes=(rwef,))
                for tt in range(NTT):
                    for dh in range(2):
                        pp = (tt * 2 + dh) % 2
                        self.mm(rpsT[pp], psT[pp][:, :],
                                wef[0:8, tt * 128:(tt + 1) * 128],
                                b2b[0:8, dh * 512:(dh + 1) * 512],
                                (rwef, rb2b), True, True)
                        em.op("vector", "tensor_copy",
                              acc[:, tt * D + dh * 512: tt * D + (dh + 1) * 512],
                              psT[pp][:, :], reads=(rpsT[pp],), writes=(racc,))
                self.barrier()

            with ExitStack() as sm:
                xn2 = sm.enter_context(nc.sbuf_tensor("m_xn2", [128, DC * HALF], BF16))
                hwin = sm.enter_context(nc.sbuf_tensor("m_hwin", [128, 16 * HALF // 1], BF16)) \
                    if False else sm.enter_context(nc.sbuf_tensor("m_hwin", [128, 16 * 1024], BF16))
                w1b = [sm.enter_context(nc.sbuf_tensor(f"m_w1b{k}", [128, DC * 128], BF16)) for k in range(2)]
                w2b = [sm.enter_context(nc.sbuf_tensor(f"m_w2b{k}", [128, D], BF16)) for k in range(4)]
                b1c = sm.enter_context(nc.sbuf_tensor("m_b1c", [128, HC], F32))
                xnt = sm.enter_context(nc.sbuf_tensor("m_xnt", [128, D], F32))
                psY = [sm.enter_context(nc.psum_tensor(f"m_psY{k}", [128, 512], F32)) for k in range(8)]
                rxn2, rhwin, rb1 = (R("m_xn2"), R("m_hwin"), R("m_b1c"))
                rw1b = [R("m_w1b", k) for k in range(2)]
                rw2b2 = [R("m_w2b", k) for k in range(4)]
                rxnt = R("m_xnt")
                rpsY = [R("m_psY", k) for k in range(8)]
                for half in range(2):
                    self.emit_xn_fm(half * 8, 8, xnt, rxnt, xn2, rxn2, HALF,
                                    psY[0], rpsY[0], psY[1], rpsY[1])
                    for e in range(NE):
                        self.dma_in(b1c[:, :], d["L1_ffn_b1"][e].rearrange("(c p) -> p c", p=128),
                                    writes=(rb1,), slow=True)
                        for mg in range(2):
                            for ml in range(16):
                                m = mg * 16 + ml
                                p = ml % 2
                                self.dma_in(w1b[p][:, :],
                                            d["w1e_b"][e][:, m * DC * 128:(m + 1) * DC * 128],
                                            writes=(rw1b[p],), chan=self.cW[p],
                                            extra_waits=tuple(R("mp", "dram").wt.items()))
                                for ns in range(2):
                                    pp = ns % 2
                                    for dd in range(DC):
                                        self.mm(rpsY[pp], psY[pp][:, :],
                                                w1b[p][:, dd * 128:(dd + 1) * 128],
                                                xn2[:, dd * HALF + ns * 512: dd * HALF + (ns + 1) * 512],
                                                (rw1b[p], rxn2), dd == 0, dd == DC - 1)
                                    em.op("scalar", "activation",
                                          hwin[:, ml * 1024 + ns * 512: ml * 1024 + (ns + 1) * 512],
                                          psY[pp][:, :], ACT.Silu, bias=b1c[:, m:m + 1],
                                          reads=(rpsY[pp], rb1), writes=(rhwin,))
                            for ttb in range(2):
                                for ml in range(16):
                                    m = mg * 16 + ml
                                    p = ml % 4
                                    self.dma_in(w2b[p][:, :], d["w2e_b"][e][m * 128:(m + 1) * 128, :],
                                                writes=(rw2b2[p],), chan=self.cW[2 + p],
                                                extra_waits=tuple(R("mp", "dram").wt.items()))
                                    for tl in range(4):
                                        for dh in range(2):
                                            self.mm(rpsY[tl * 2 + dh], psY[tl * 2 + dh][:, :],
                                                    hwin[:, ml * 1024 + (ttb * 4 + tl) * 128: ml * 1024 + (ttb * 4 + tl) * 128 + 128],
                                                    w2b[p][:, dh * 512:(dh + 1) * 512],
                                                    (rhwin, rw2b2[p]), ml == 0, ml == 15)
                                for tl in range(4):
                                    tt = half * 8 + ttb * 4 + tl
                                    for dh in range(2):
                                        em.op("vector", "scalar_tensor_tensor",
                                              acc[:, tt * D + dh * 512: tt * D + (dh + 1) * 512],
                                              psY[tl * 2 + dh][:, :],
                                              we[:, tt * NE + e: tt * NE + e + 1],
                                              acc[:, tt * D + dh * 512: tt * D + (dh + 1) * 512],
                                              ALU.mult, ALU.add,
                                              reads=(rpsY[tl * 2 + dh], rwe, racc), writes=(racc,))
                self.barrier()
            for tt in range(NTT):
                em.op("scalar", "activation", acc[:, tt * D:(tt + 1) * D],
                      acc[:, tt * D:(tt + 1) * D], ACT.Silu, reads=(racc,), writes=(racc,))
            self.tap("g2", acc)

            K1 = s1.enter_context(nc.sbuf_tensor("K1", [128, 2 * T], BF16))
            Q1 = s1.enter_context(nc.sbuf_tensor("Q1", [128, 2 * HALF], BF16))
            v_tm = s1.enter_context(nc.sbuf_tensor("v_tm", [128, NTT * D], BF16))
            rK1, rQ1, rvtm = R("K1"), R("Q1"), R("v_tm")
            for half in range(2):
                with ExitStack() as sp:
                    stg = sp.enter_context(nc.sbuf_tensor("p_stg", [128, D], F32))
                    wb = sp.enter_context(nc.sbuf_tensor("p_wb", [128, DC * D1], BF16))
                    wvub = sp.enter_context(nc.sbuf_tensor("p_wvub", [128, 2 * D], BF16))
                    V2 = sp.enter_context(nc.sbuf_tensor("p_V2", [128, 2 * HALF], BF16))
                    xnq = sp.enter_context(nc.sbuf_tensor("p_xnq", [128, DC * QB], BF16))
                    xnt = sp.enter_context(nc.sbuf_tensor("p_xnt", [128, D], F32))
                    psA = [sp.enter_context(nc.psum_tensor(f"p_psA{k}", [128, 512], F32)) for k in range(2)]
                    rstg, rwb, rwvub, rV2 = R("p_stg"), R("p_wb"), R("p_wvub"), R("p_V2")
                    rxnq, rxnt = R("p_xnq"), R("p_xnt")
                    rpsA = [R("p_psA", k) for k in range(2)]
                    for wname, dst_ap, dst_res, outcols in [
                        ("L1_attn_Wk", K1, rK1, "K"), ("L1_attn_Wq", Q1, rQ1, "Q"),
                        ("L1_attn_Wvdown", V2, rV2, "V2")]:
                        for dd2 in range(4):
                            self.dma_in(stg[:, :].rearrange("p (dd c) -> p dd c", c=D1)[:, 0:2, :],
                                        d[wname].rearrange("(dd p) c -> p dd c", p=128)[:, dd2 * 2:(dd2 + 1) * 2, :],
                                        writes=(rstg,))
                            em.op("vector", "tensor_copy",
                                  wb[:, dd2 * 2 * D1:(dd2 * 2 + 2) * D1], stg[:, 0:2 * D1],
                                  reads=(rstg,), writes=(rwb,))
                        for ql in range(2):
                            qb = half * 2 + ql
                            self.emit_xn_fm(qb * 4, 4, xnt, rxnt, xnq, rxnq, QB,
                                            psA[0], rpsA[0], psA[1], rpsA[1])
                            for mc in range(2):
                                pp = mc % 2
                                for dd in range(DC):
                                    self.mm(rpsA[pp], psA[pp][:, :],
                                            wb[:, dd * D1 + mc * 128: dd * D1 + (mc + 1) * 128],
                                            xnq[:, dd * QB:(dd + 1) * QB],
                                            (rwb, rxnq), dd == 0, dd == DC - 1)
                                if outcols == "K":
                                    em.op("scalar", "copy",
                                          dst_ap[:, mc * T + qb * 512: mc * T + (qb + 1) * 512],
                                          psA[pp][:, :], reads=(rpsA[pp],), writes=(dst_res,))
                                elif outcols == "Q":
                                    em.op("scalar", "activation",
                                          dst_ap[:, mc * HALF + ql * 512: mc * HALF + (ql + 1) * 512],
                                          psA[pp][:, :], ACT.Copy, scale=0.0625,
                                          reads=(rpsA[pp],), writes=(dst_res,))
                                else:
                                    em.op("scalar", "copy",
                                          dst_ap[:, mc * HALF + ql * 512: mc * HALF + (ql + 1) * 512],
                                          psA[pp][:, :], reads=(rpsA[pp],), writes=(dst_res,))
                    for c2 in range(2):
                        self.dma_in(stg[:, 0:D].rearrange("p (c v) -> p c v", c=1),
                                    d["L1_attn_Wvup"].rearrange("(c p) v -> p c v", p=128)[:, c2:c2 + 1, :],
                                    writes=(rstg,))
                        em.op("vector", "tensor_copy", wvub[:, c2 * D:(c2 + 1) * D], stg[:, 0:D],
                              reads=(rstg,), writes=(rwvub,))
                    for tl in range(8):
                        tt = half * 8 + tl
                        for bk in range(2):
                            pp = bk % 2
                            for c2 in range(2):
                                self.mm(rpsA[pp], psA[pp][:, :],
                                        V2[:, c2 * HALF + tl * 128: c2 * HALF + (tl + 1) * 128],
                                        wvub[:, c2 * D + bk * 512: c2 * D + (bk + 1) * 512],
                                        (rV2, rwvub), c2 == 0, c2 == 1)
                            em.op("scalar", "copy",
                                  v_tm[:, tt * D + bk * 512: tt * D + (bk + 1) * 512],
                                  psA[pp][:, :], reads=(rpsA[pp],), writes=(rvtm,))
                    self.barrier()
                with ExitStack() as sb:
                    pT1 = sb.enter_context(nc.sbuf_tensor("t_pT", [128, 16 * 512], BF16))
                    zr = sb.enter_context(nc.sbuf_tensor("t_zr", [1, 512], F32))
                    zc = sb.enter_context(nc.sbuf_tensor("t_zc", [128, 4], F32))
                    rzc2 = sb.enter_context(nc.sbuf_tensor("t_rzc", [128, 4], F32))
                    ub = sb.enter_context(nc.sbuf_tensor("t_ub", [128, 512], F32))
                    psS = [sb.enter_context(nc.psum_tensor(f"t_psS{k}", [128, 512], F32)) for k in range(2)]
                    psZ = sb.enter_context(nc.psum_tensor("t_psZ", [128, 512], F32))
                    psA2 = [sb.enter_context(nc.psum_tensor(f"t_psA{k}", [128, 512], F32)) for k in range(2)]
                    rpT1, rzr, rzc, rrzc, rub = R("t_pT"), R("t_zr"), R("t_zc"), R("t_rzc"), R("t_ub")
                    rpsS = [R("t_psS", k) for k in range(2)]
                    rpsZ = R("t_psZ")
                    rpsA2 = [R("t_psA", k) for k in range(2)]
                    for ql in range(2):
                        qb = half * 2 + ql
                        nkc = qb * 4 + 4
                        for kc in range(nkc):
                            ss = kc % 2
                            for c2 in range(2):
                                self.mm(rpsS[ss], psS[ss][:, :],
                                        K1[:, c2 * T + kc * 128:(c2 * T) + (kc + 1) * 128],
                                        Q1[:, c2 * HALF + ql * 512: c2 * HALF + (ql + 1) * 512],
                                        (rK1, rQ1), c2 == 0, c2 == 1)
                            em.op("scalar", "activation", pT1[:, kc * 512:(kc + 1) * 512],
                                  psS[ss][:, :], ACT.Exp, reads=(rpsS[ss],), writes=(rpT1,))
                            j = kc - qb * 4
                            if j >= 0:
                                em.op("vector", "tensor_mul", pT1[:, kc * 512:(kc + 1) * 512],
                                      pT1[:, kc * 512:(kc + 1) * 512],
                                      self.masks[:, j * 512:(j + 1) * 512],
                                      reads=(rpT1, R("masks")), writes=(rpT1,))
                            self.mm(rpsZ, psZ[0:1, :], self.ones_col_bf[:, 0:1],
                                    pT1[:, kc * 512:(kc + 1) * 512],
                                    (R("ones"), rpT1), kc == 0, kc == nkc - 1)
                        em.op("vector", "tensor_copy", zr[0:1, :], psZ[0:1, :],
                              reads=(rpsZ,), writes=(rzr,))
                        for s in range(4):
                            em.op("tensor", "transpose", psZ[:, 0:1],
                                  zr[0:1, s * 128:(s + 1) * 128], self.identf[0:1, 0:1],
                                  reads=(rzr, R("ident")), writes=(rpsZ,))
                            em.op("vector", "tensor_copy", zc[:, s:s + 1], psZ[:, 0:1],
                                  reads=(rpsZ,), writes=(rzc,))
                        em.op("vector", "reciprocal", rzc2[:, :], zc[:, :],
                              reads=(rzc,), writes=(rrzc,))
                        for qs in range(4):
                            tt = qb * 4 + qs
                            for dh in range(2):
                                pa = dh % 2
                                for kc in range(nkc):
                                    self.mm(rpsA2[pa], psA2[pa][:, :],
                                            pT1[:, kc * 512 + qs * 128: kc * 512 + (qs + 1) * 128],
                                            v_tm[:, kc * D + dh * 512: kc * D + (dh + 1) * 512],
                                            (rpT1, rvtm), kc == 0, kc == nkc - 1)
                                em.op("vector", "scalar_tensor_tensor", ub[:, :],
                                      psA2[pa][:, :], rzc2[:, qs:qs + 1],
                                      acc[:, tt * D + dh * 512: tt * D + (dh + 1) * 512],
                                      ALU.mult, ALU.mult,
                                      reads=(rpsA2[pa], rrzc, racc), writes=(rub,))
                                em.op("vector", "tensor_add",
                                      self.x[:, tt * D + dh * 512: tt * D + (dh + 1) * 512],
                                      self.x[:, tt * D + dh * 512: tt * D + (dh + 1) * 512],
                                      ub[:, :], reads=(rub, R("x")), writes=(R("x"),))
                    self.barrier()

    def tap(self, name, sbuf_t, rows=128):
        if name not in self.debug_taps:
            return
        nc, em = self.nc, self.em
        ap = nc.dram_tensor(f"tap_{name}", [rows, sbuf_t.shape[1]], sbuf_t.dtype,
                            kind="ExternalOutput").ap()
        self.tap_aps[name] = ap
        self.dma_out(ap[:, :], sbuf_t[0:rows, :], reads=(),
                     extra_waits=em.barrier_tok())
        em.wait_toks("gpsimd", [(self.cG, self.cG.n)])
        em.flush()

    def phase_out(self):
        d, em = self.d, self.em
        oout = d["out"].rearrange("(n p) d -> p n d", p=128)
        xsrc = self.x[:, :].rearrange("p (n d) -> p n d", d=D)
        self.dma_out(oout, xsrc, reads=(self.R("x"),), extra_waits=em.barrier_tok())
        em.wait_toks("gpsimd", [(self.cG, self.cG.n)])
        em.flush()


PAIR_PERM = np.concatenate([
    128 * c + 64 * h + np.concatenate([np.arange(0, 64, 2), np.arange(1, 64, 2)])
    for c in range(8) for h in range(2)])


def _derived_inputs(inputs):
    m = {}
    m["Wq_p"] = np.ascontiguousarray(np.asarray(inputs["L0_attn_Wq"])[PAIR_PERM])
    m["Wk_p"] = np.ascontiguousarray(np.asarray(inputs["L0_attn_Wk"])[PAIR_PERM])
    m["bq_p"] = np.ascontiguousarray(np.asarray(inputs["L0_attn_bq"])[PAIR_PERM])
    m["bk_p"] = np.ascontiguousarray(np.asarray(inputs["L0_attn_bk"])[PAIR_PERM])
    rw1 = np.asarray(inputs["L1_ffn_rw1"], np.float32)
    m["rw1_p"] = np.ascontiguousarray(
        rw1.reshape(8, 128, 32, 128).transpose(1, 2, 0, 3).reshape(128, 32 * 8 * 128))
    w1e = np.asarray(inputs["L1_ffn_w1"], np.float32)
    m["w1e_p"] = np.ascontiguousarray(
        w1e.reshape(8, 8, 128, 32, 128).transpose(0, 2, 3, 1, 4).reshape(8, 128, 32 * 8 * 128))
    return m


_CONSTS = None


def kernel(**inputs):
    global _CONSTS
    if _CONSTS is None:
        _CONSTS = _host_consts()
    mk = MK()
    nc = mk.build()
    xs = np.ascontiguousarray(inputs["L0_attn_x"])
    base = {k: np.ascontiguousarray(v) for k, v in inputs.items()
            if k not in ("L0_attn_x", "L0_attn_token_positions")}
    base.update(_CONSTS)
    base.update(_derived_inputs(inputs))
    in_maps = []
    for c in range(8):
        m = dict(base)
        m["x"] = xs[c]
        in_maps.append(m)
    res = run_bass_kernel_spmd(nc, in_maps, core_ids=list(range(8)))
    out = np.stack([np.asarray(r["out"]) for r in res.results], axis=0)
    return out.astype(np.float32)


# version 12
# speedup vs baseline: 1.0647x; 1.0647x over previous
import numpy as np
import ml_dtypes
from contextlib import ExitStack

import concourse.bass as bass
import concourse.mybir as mybir
from concourse.bass_utils import run_bass_kernel_spmd

F32 = mybir.dt.float32
BF16 = mybir.dt.bfloat16
ALU = mybir.AluOpType
ACT = mybir.ActivationFunctionType
AX = mybir.AxisListType

T = 2048
D = 1024
HID = 4096
NH = 16
DK = 64
NE = 8
DC = D // 128
HC = HID // 128
NTT = T // 128
QB = 512
NQB = T // QB
HALF = T // 2
FQB = 128
NFQB_H = HALF // FQB
D1 = 256

NEG_TIE = -1e30


def _host_consts():
    c = {}
    ident = np.zeros((128, 128), dtype=ml_dtypes.bfloat16)
    np.fill_diagonal(ident, ml_dtypes.bfloat16(1.0))
    c["c_ident"] = ident
    identf = np.zeros((128, 128), dtype=np.float32)
    np.fill_diagonal(identf, 1.0)
    c["c_identf"] = identf
    masks = np.zeros((128, 4, 512), dtype=ml_dtypes.bfloat16)
    for j in range(4):
        k = np.arange(128)[:, None]
        q = np.arange(512)[None, :]
        masks[:, j, :] = (q >= j * 128 + k).astype(ml_dtypes.bfloat16)
    c["c_masks"] = np.ascontiguousarray(masks.reshape(128, 4 * 512))
    inv_freq = (1.0 / (np.float32(10000.0) ** (np.arange(0, DK, 2, dtype=np.float32) / np.float32(DK)))).astype(np.float32)
    tpos = np.arange(T, dtype=np.float32)
    ang = (tpos[:, None] * inv_freq[None, :]).astype(np.float32)
    cos = np.cos(ang).astype(np.float32)
    sin = np.sin(ang).astype(np.float32)
    A = np.concatenate([cos.T, cos.T, cos.T, cos.T], axis=0)
    Bt = np.concatenate([-sin.T, sin.T, -sin.T, sin.T], axis=0)
    c["c_ropeA"] = np.ascontiguousarray(A, dtype=np.float32)
    c["c_ropeB"] = np.ascontiguousarray(Bt, dtype=np.float32)
    return c


class Chan:
    def __init__(self, es, nc, name, dma=False, slotted=False):
        self.sem = es.enter_context(nc.semaphore(name))
        self.n = 0
        self.name = name
        self.dma = dma
        self.slotted = slotted
        self.maxwaited = 0

    def tick(self):
        self.n += 16 if self.dma else 1
        return self.n


class Res:
    __slots__ = ("wt", "rd", "name")

    def __init__(self, name=""):
        self.wt = {}
        self.rd = {}
        self.name = name


class Emit:
    ENGINES = ("sync", "tensor", "vector", "scalar", "gpsimd")

    def __init__(self, block, chans):
        self.block = block
        self.ch = chans
        self.qs = {e: [] for e in self.ENGINES}
        self.lastw = {e: {} for e in self.ENGINES}

    def _wait_list(self, eng, toks):
        wl = []
        for (ch, n) in sorted(toks, key=lambda t: t[0].name):
            if n <= 0:
                continue
            assert n <= ch.n, f"wait {ch.name}>={n} but only {ch.n} emitted"
            if ch.dma and not ch.slotted:
                n = ch.n
            if ch.dma:
                ch.maxwaited = max(ch.maxwaited, n)
            if self.lastw[eng].get(id(ch), -1) >= n:
                continue
            self.lastw[eng][id(ch)] = n
            wl.append((ch.sem, n))
        return wl

    def op(self, eng, method, *args, reads=(), writes=(), chan=None,
           psum_start=True, psum_stop=True, extra_waits=(), **kw):
        ch = chan if chan is not None else self.ch[eng]
        toks = list(extra_waits)
        if ch.dma and ch.maxwaited > 0:
            toks.append((ch, min(ch.maxwaited, ch.n)))
        for r in reads:
            toks += list(r.wt.items())
        for w in writes:
            toks += list(w.wt.items())
            toks += list(w.rd.items())
        wl = self._wait_list(eng, toks)
        n = ch.tick()
        tok = (ch, n)
        for r in reads:
            r.rd[ch] = n
        for w in writes:
            if psum_stop:
                w.wt = {ch: n}
                w.rd = {}
        def fn(e, method=method, args=args, kw=kw, wl=wl, ch=ch):
            for (sem, v) in wl:
                e.wait_ge(sem, v)
            inst = getattr(e, method)(*args, **kw)
            inst.then_inc(ch.sem, 16 if ch.dma else 1)
            return inst
        self.qs[eng].append(fn)
        return tok

    def mm(self, psum_res, out_ap, lhsT, rhs, reads, start, stop, **kw):
        return self.op(
            "tensor", "matmul", out_ap, lhsT, rhs,
            reads=reads,
            writes=(psum_res,) if (start or stop) else (psum_res,),
            psum_stop=stop,
            start=start, stop=stop,
            **kw) if start else self._mm_mid(psum_res, out_ap, lhsT, rhs, reads, stop, **kw)

    def _mm_mid(self, psum_res, out_ap, lhsT, rhs, reads, stop, **kw):
        ch = self.ch["tensor"]
        toks = []
        for r in reads:
            toks += list(r.wt.items())
        wl = self._wait_list("tensor", toks)
        n = ch.tick()
        for r in reads:
            r.rd[ch] = n
        if stop:
            psum_res.wt = {ch: n}
            psum_res.rd = {}
        def fn(e, wl=wl, out_ap=out_ap, lhsT=lhsT, rhs=rhs, kw=kw, stop=stop, ch=ch):
            for (sem, v) in wl:
                e.wait_ge(sem, v)
            inst = e.matmul(out_ap, lhsT, rhs, start=False, stop=stop, **kw)
            inst.then_inc(ch.sem, 1)
            return inst
        self.qs["tensor"].append(fn)
        return (ch, n)

    def wait_toks(self, eng, toks):
        wl = self._wait_list(eng, toks)
        if wl:
            self.qs[eng].append(lambda e, wl=wl: [e.wait_ge(s, v) for (s, v) in wl])

    def barrier_tok(self):
        seen = {}
        for chs in self._all_chans:
            seen[chs] = chs.n
        return [(c, n) for c, n in seen.items()]

    def flush(self):
        for eng in self.ENGINES:
            fns = self.qs[eng]
            if fns:
                getattr(self.block, eng)(lambda e, fns=tuple(fns): [f(e) for f in fns])
        self.qs = {e: [] for e in self.ENGINES}
        self.lastw = {e: {} for e in self.ENGINES}


class MK:
    def __init__(self, debug_taps=(), upto=None):
        self.debug_taps = list(debug_taps)
        self.upto = upto
        self.tap_aps = {}

    def build(self):
        nc = bass.Bass("TRN2", target_bir_lowering=False, debug=False)
        self.nc = nc
        es = ExitStack()
        self.es = es

        d = {}
        d["x"] = nc.dram_tensor("x", [T, D], F32, kind="ExternalInput").ap()
        for nm, shp in [
            ("L0_attn_Wq", [D, D]), ("L0_attn_Wk", [D, D]), ("L0_attn_Wv", [D, D]),
            ("L0_attn_Wo", [D, D]),
            ("L0_attn_bq", [D]), ("L0_attn_bk", [D]), ("L0_attn_bv", [D]), ("L0_attn_bo", [D]),
            ("L0_ffn_w1", [HID, D]), ("L0_ffn_w2", [D, HID]),
            ("L0_ffn_conv_w", [HID, 1, 3]), ("L0_ffn_conv_b", [HID]),
            ("L1_attn_Wq", [D, D1]), ("L1_attn_Wk", [D, D1]),
            ("L1_attn_Wvdown", [D, D1]), ("L1_attn_Wvup", [D1, D]),
            ("L1_ffn_rw1", [D, HID]), ("L1_ffn_rb1", [HID]),
            ("L1_ffn_rw2", [HID, NE]), ("L1_ffn_rb2", [NE]),
            ("L1_ffn_w1", [NE, D, HID]), ("L1_ffn_b1", [NE, HID]),
            ("L1_ffn_w2", [NE, HID, D]), ("L1_ffn_b2", [NE, D]),
            ("c_ropeA", [128, T]), ("c_ropeB", [128, T]),
        ]:
            d[nm] = nc.dram_tensor(nm, shp, F32, kind="ExternalInput").ap()
        for nm, shp in [("Wq_p", [D, D]), ("Wk_p", [D, D]), ("bq_p", [D]), ("bk_p", [D]),
                        ("rw1_p", [128, HC * DC * 128]), ("w1e_p", [NE, 128, HC * DC * 128])]:
            d[nm] = nc.dram_tensor(nm, shp, F32, kind="ExternalInput").ap()
        d["c_ident"] = nc.dram_tensor("c_ident", [128, 128], BF16, kind="ExternalInput").ap()
        d["c_identf"] = nc.dram_tensor("c_identf", [128, 128], F32, kind="ExternalInput").ap()
        d["c_masks"] = nc.dram_tensor("c_masks", [128, 4 * 512], BF16, kind="ExternalInput").ap()
        d["out"] = nc.dram_tensor("out", [T, D], F32, kind="ExternalOutput").ap()
        d["w1qT_d"] = nc.dram_tensor("w1qT_d", [128, (HID // 128) * D], BF16, kind="Internal").ap()
        d["w1e_b"] = nc.dram_tensor("w1e_b", [NE, 128, HC * DC * 128], BF16, kind="Internal").ap()
        d["w2e_b"] = nc.dram_tensor("w2e_b", [NE, HID, D], BF16, kind="Internal").ap()
        d["w2qT_d"] = nc.dram_tensor("w2qT_d", [HID, D], BF16, kind="Internal").ap()
        d["WoT_d"] = nc.dram_tensor("WoT_d", [D, D], BF16, kind="Internal").ap()
        self.d = d

        self._uid = [0]
        def _uniq(name):
            self._uid[0] += 1
            return f"{name}__{self._uid[0]}"
        _orig_sb = nc.sbuf_tensor
        _orig_ps = nc.psum_tensor
        nc.sbuf_tensor = lambda name, shape, dt: _orig_sb(_uniq(name), shape, dt)
        nc.psum_tensor = lambda name, shape, dt: _orig_ps(_uniq(name), shape, dt)
        sbt = lambda name, shape, dt: es.enter_context(nc.sbuf_tensor(name, shape, dt))
        self.sbt = sbt
        self.x = sbt("x_tm", [128, NTT * D], F32)
        self.ident = sbt("ident", [128, 128], BF16)
        self.identf = sbt("identf", [128, 128], F32)
        self.masks = sbt("masks", [128, 4 * 512], BF16)
        self.mcols = sbt("mcols", [128, NTT], F32)
        self.icols = sbt("icols", [128, NTT], F32)
        self.gcols = sbt("gcols", [128, 8], F32)
        self.cxcols = sbt("cxcols", [128, NTT], F32)
        self.ones_row_bf = sbt("ones_row_bf", [1, 128], BF16)
        self.ones_row_f = sbt("ones_row_f", [64, 128], F32)
        self.ones_col_bf = sbt("ones_col_bf", [128, 1], BF16)
        self.ones_col_f = sbt("ones_col_f", [128, 1], F32)

        self.cT = Chan(es, nc, "cT")
        self.cV = Chan(es, nc, "cV")
        self.cA = Chan(es, nc, "cA")
        self.cP = Chan(es, nc, "cP")
        self.cS = Chan(es, nc, "cS", dma=True)
        self.cW = [Chan(es, nc, f"cW{i}", dma=True, slotted=True) for i in range(6)]
        self.cG = Chan(es, nc, "cG", dma=True)
        self.cG2 = Chan(es, nc, "cG2", dma=True)
        self._dma_chans = [self.cS, self.cG, self.cG2] + self.cW

        self.block = es.enter_context(nc.Block())
        em = Emit(self.block, {"tensor": self.cT, "vector": self.cV,
                               "scalar": self.cA, "gpsimd": self.cP,
                               "sync": self.cS})
        em._all_chans = [self.cT, self.cV, self.cA, self.cP, self.cS, self.cG, self.cG2] + self.cW
        self.em = em

        self._res = {}
        self.R = lambda *key: self._res.setdefault(key, Res(str(key)))

        with es:
            self._emit_all()
        return nc

    def mp_init(self):
        d = self.d
        self._mp = []
        self._mp_i = 0
        self._mp_sid = 0
        for e in range(NE):
            for k in range(HC):
                self._mp.append((d["w1e_b"][e][:, k * 1024:(k + 1) * 1024],
                                 d["w1e_p"][e][:, k * 1024:(k + 1) * 1024]))
            for m in range(HC):
                self._mp.append((d["w2e_b"][e][m * 128:(m + 1) * 128, :],
                                 d["L1_ffn_w2"][e, m * 128:(m + 1) * 128, :]))

    def mp_scope(self, ss, nbuf=2):
        nc, R = self.nc, self.R
        self._mp_sid += 1
        sid = self._mp_sid
        stg = [ss.enter_context(nc.sbuf_tensor(f"mp_stg{k}", [128, 1024], F32)) for k in range(nbuf)]
        bf = [ss.enter_context(nc.sbuf_tensor(f"mp_bf{k}", [128, 1024], BF16)) for k in range(nbuf)]
        return {"stg": stg, "bf": bf, "nbuf": nbuf, "pend": [],
                "rstg": [R("mp", "stg", sid, k) for k in range(nbuf)],
                "rbf": [R("mp", "bf", sid, k) for k in range(nbuf)]}

    def _mp_cast(self, st, i, j):
        dst = self._mp[i][0]
        if i % 2 == 0:
            self.em.op("vector", "tensor_copy", st["bf"][j][:, :], st["stg"][j][:, :],
                       reads=(st["rstg"][j],), writes=(st["rbf"][j],))
        else:
            self.em.op("scalar", "copy", st["bf"][j][:, :], st["stg"][j][:, :],
                       reads=(st["rstg"][j],), writes=(st["rbf"][j],))
        self.dma_out(dst, st["bf"][j][:, :], reads=(st["rbf"][j],),
                     writes=(self.R("mp", "dram"),))

    def mp_step(self, st, n):
        if st is None:
            return
        for _ in range(min(n, len(st["pend"]))):
            self._mp_cast(st, *st["pend"].pop(0))
        free = st["nbuf"] - len(st["pend"])
        for _ in range(min(n, free)):
            if self._mp_i >= len(self._mp):
                return
            i = self._mp_i
            self._mp_i += 1
            j = i % st["nbuf"]
            self.dma_in(st["stg"][j][:, :], self._mp[i][1], writes=(st["rstg"][j],),
                        chan=self.cW[4 + (i % 2)])
            st["pend"].append((i, j))

    def mp_flush(self, st):
        if st is None:
            return
        while st["pend"]:
            self._mp_cast(st, *st["pend"].pop(0))

    def barrier(self):
        toks = self.em.barrier_tok()
        for eng in Emit.ENGINES:
            self.em.wait_toks(eng, toks)

    def dma_in(self, dst_ap, src_ap, writes, reads=(), chan=None, extra_waits=(),
               slow=False):
        kw = {"allow_slow_non_contiguous": True} if slow else {}
        return self.em.op("sync", "dma_start", chan=chan or self.cS,
                          reads=reads, writes=writes, extra_waits=extra_waits,
                          out=dst_ap, in_=src_ap, **kw)

    def dma_out(self, dst_ap, src_ap, reads, writes=(), extra_waits=(), eng="gpsimd"):
        return self.em.op(eng, "dma_start",
                          chan=self.cG if eng == "gpsimd" else self.cG2,
                          reads=reads, writes=writes, extra_waits=extra_waits,
                          out=dst_ap, in_=src_ap)

    def _emit_all(self):
        self.mp_init()
        self.phase_load()
        if self.upto == "load":
            return self.phase_out()
        self.phase_stats(0)
        if self.upto == "stats0":
            return self.phase_out()
        self.phase_wprep()
        if self.upto == "wprep":
            return self.phase_out()
        self.phase_l0()
        if self.upto == "l0":
            return self.phase_out()
        self.phase_stats(1)
        self.phase_l1()
        self.phase_out()

    def phase_load(self):
        d, em = self.d, self.em
        xin = d["x"].rearrange("(n p) d -> p n d", p=128)
        xdst = self.x[:, :].rearrange("p (n d) -> p n d", d=D)
        self.dma_in(xdst, xin, writes=(self.R("x"),))
        self.dma_in(self.ident[:, :], d["c_ident"][:, :], writes=(self.R("ident"),))
        self.dma_in(self.identf[:, :], d["c_identf"][:, :], writes=(self.R("ident"),))
        self.dma_in(self.masks[:, :], d["c_masks"][:, :], writes=(self.R("masks"),))
        em.op("vector", "memset", self.gcols[:, :], 0.0, writes=(self.R("gcols"),))
        em.op("vector", "memset", self.ones_row_bf[:, :], 1.0, writes=(self.R("ones"),))
        em.op("vector", "memset", self.ones_row_f[:, :], 1.0, writes=(self.R("ones"),))
        em.op("vector", "memset", self.ones_col_bf[:, :], 1.0, writes=(self.R("ones"),))
        em.op("vector", "memset", self.ones_col_f[:, :], 1.0, writes=(self.R("ones"),))
        em.flush()

    def phase_stats(self, layer):
        nc, em, R = self.nc, self.em, self.R
        with ExitStack() as ss:
            cen = ss.enter_context(nc.sbuf_tensor(f"st_cen{layer}", [128, D], F32))
            sq = ss.enter_context(nc.sbuf_tensor(f"st_sq{layer}", [128, D], F32))
            ssm = ss.enter_context(nc.sbuf_tensor(f"st_ss{layer}", [128, NTT], F32))
            tmp = ss.enter_context(nc.sbuf_tensor(f"st_tmp{layer}", [128, NTT], F32))
            rcen, rsq, rss, rtmp = R("st_cen"), R("st_sq"), R("st_ss"), R("st_tmp")
            rm, ri = R("mcols"), R("icols")
            for t in range(NTT):
                xs = self.x[:, t * D:(t + 1) * D]
                em.op("vector", "reduce_sum", tmp[:, t:t + 1], xs, AX.X,
                      reads=(R("x"),), writes=(rtmp,))
                em.op("vector", "tensor_scalar", self.mcols[:, t:t + 1], tmp[:, t:t + 1],
                      1.0 / D, None, ALU.mult, reads=(rtmp,), writes=(rm,))
                em.op("vector", "tensor_scalar", cen[:, :], xs, self.mcols[:, t:t + 1],
                      None, ALU.subtract, reads=(R("x"), rm), writes=(rcen,))
                em.op("scalar", "activation", sq[:, :], cen[:, :], ACT.Square,
                      reads=(rcen,), writes=(rsq, rss), accum_out=ssm[:, t:t + 1])
            em.op("vector", "tensor_scalar", ssm[:, :], ssm[:, :], 1.0 / D, 1e-5,
                  ALU.mult, ALU.add, reads=(rss,), writes=(rss,))
            em.op("vector", "reciprocal", tmp[:, :], ssm[:, :], reads=(rss,), writes=(rtmp,))
            em.op("scalar", "activation", self.icols[:, :], tmp[:, :], ACT.Sqrt,
                  reads=(rtmp,), writes=(ri,))
            em.op("vector", "tensor_mul", tmp[:, :], self.icols[:, :], self.icols[:, :],
                  reads=(ri,), writes=(rtmp,))
            em.op("vector", "tensor_mul", tmp[:, :], tmp[:, :], ssm[:, :],
                  reads=(rtmp, rss), writes=(rtmp,))
            em.op("vector", "tensor_scalar", tmp[:, :], tmp[:, :], -0.5, 1.5,
                  ALU.mult, ALU.add, reads=(rtmp,), writes=(rtmp,))
            em.op("vector", "tensor_mul", self.icols[:, :], self.icols[:, :], tmp[:, :],
                  reads=(ri, rtmp), writes=(ri,))
            self.barrier()
            em.flush()
        self.tap("stats_m", self.mcols)
        self.tap("stats_i", self.icols)

    def emit_xn_tile(self, t, dst, dst_res):
        self.em.op("vector", "tensor_scalar", dst[:, :],
                   self.x[:, t * D:(t + 1) * D],
                   self.mcols[:, t:t + 1], self.icols[:, t:t + 1],
                   ALU.subtract, ALU.mult,
                   reads=(self.R("x"), self.R("mcols"), self.R("icols")),
                   writes=(dst_res,))

    def emit_transpose_tile(self, src, src_res, dst_cols, ps, psr, dtype_bf=True):
        em = self.em
        for c in range(DC):
            pr = psr[c % 2]
            em.op("tensor", "transpose", ps[c % 2][:, 0:128], src[:, c * 128:(c + 1) * 128],
                  self.ident[:, :], reads=(src_res, self.R("ident")), writes=(pr,))
            dap, dres = dst_cols(c)
            em.op("vector", "tensor_copy", dap, ps[c % 2][:, 0:128],
                  reads=(pr,), writes=(dres,))

    def _build_wT(self, tag, src, dst, O, I, sca_col=None, ss=None, pm=False, mp=None,
                  st_eng="gpsimd"):
        nc, em, R = self.nc, self.em, self.R
        stg = [ss.enter_context(nc.sbuf_tensor(f"{tag}_st{k}", [128, I], F32)) for k in range(2)]
        wq = [ss.enter_context(nc.sbuf_tensor(f"{tag}_wq{k}", [128, I], BF16)) for k in range(2)]
        ntb = 8
        if pm:
            tbw = [ss.enter_context(nc.sbuf_tensor(f"{tag}_tw{k}", [128, (I // 128) * 128], BF16)) for k in range(2)]
            rtbw = [R(tag, "tw", k) for k in range(2)]
        else:
            tb = [ss.enter_context(nc.sbuf_tensor(f"{tag}_tb{k}", [128, 128], BF16)) for k in range(ntb)]
        gl = ss.enter_context(nc.sbuf_tensor(f"{tag}_g", [128, I], F32))
        ps = [ss.enter_context(nc.psum_tensor(f"{tag}_ps{k}", [128, 1024], BF16)) for k in range(2)]
        rstg = [R(tag, "stg", k) for k in range(2)]
        rwq = [R(tag, "wq", k) for k in range(2)]
        rtb = [R(tag, "tb", k) for k in range(ntb)] if not pm else None
        rgl = R(tag, "gl")
        rps = [R(tag, "ps", k) for k in range(2)]
        rdst = R(tag, "dram")
        ti = 0
        for oo in range(O // 128):
            p = oo % 2
            self.mp_step(mp, 2)
            self.dma_in(stg[p][:, :], src[oo * 128:(oo + 1) * 128, :], writes=(rstg[p],))
            if sca_col is not None:
                em.op("vector", "tensor_scalar", gl[:, :], stg[p][:, :], sca_col, 0.5,
                      ALU.mult, ALU.is_gt, reads=(rstg[p], R("gcols")), writes=(rgl,))
                em.op("vector", "tensor_scalar", stg[p][:, :], stg[p][:, :], sca_col, -0.5,
                      ALU.mult, ALU.is_lt, reads=(rstg[p], R("gcols")), writes=(rstg[p],))
                em.op("vector", "tensor_sub", wq[p][:, :], gl[:, :], stg[p][:, :],
                      reads=(rgl, rstg[p]), writes=(rwq[p],))
            else:
                em.op("vector", "tensor_copy", wq[p][:, :], stg[p][:, :],
                      reads=(rstg[p],), writes=(rwq[p],))
            for ii in range(I // 128):
                pp = ti % 2
                em.op("tensor", "transpose", ps[pp][:, 0:128], wq[p][:, ii * 128:(ii + 1) * 128],
                      self.ident[:, :], reads=(rwq[p], R("ident")), writes=(rps[pp],))
                if pm:
                    em.op("scalar", "copy", tbw[oo % 2][:, ii * 128:(ii + 1) * 128],
                          ps[pp][:, 0:128], reads=(rps[pp],), writes=(rtbw[oo % 2],))
                else:
                    tt = ti % ntb
                    em.op("scalar", "copy", tb[tt][:, :], ps[pp][:, 0:128],
                          reads=(rps[pp],), writes=(rtb[tt],))
                    dslc = dst[ii * 128:(ii + 1) * 128, oo * 128:(oo + 1) * 128]
                    self.dma_out(dslc, tb[tt][:, :], reads=(rtb[tt],), writes=(rdst,),
                                 eng=st_eng)
                ti += 1
            if pm:
                nI = I // 128
                self.dma_out(dst[:, oo * nI * 128:(oo + 1) * nI * 128],
                             tbw[oo % 2][:, :], reads=(rtbw[oo % 2],), writes=(rdst,))

    def _bitnet_scale(self, tag, src, O, I, gi, ss, mp=None):
        nc, em, R = self.nc, self.em, self.R
        stg = [ss.enter_context(nc.sbuf_tensor(f"{tag}_ms{k}", [128, I], F32)) for k in range(2)]
        col = ss.enter_context(nc.sbuf_tensor(f"{tag}_mc", [128, 2], F32))
        rstg = [R(tag, "ms", k) for k in range(2)]
        rcol = R(tag, "mc")
        for oo in range(O // 128):
            p = oo % 2
            self.mp_step(mp, 1)
            self.dma_in(stg[p][:, :], src[oo * 128:(oo + 1) * 128, :], writes=(rstg[p],))
            em.op("vector", "reduce_sum", col[:, 1:2], stg[p][:, :], AX.X,
                  apply_absolute_value=True, reads=(rstg[p],), writes=(rcol,))
            if oo == 0:
                em.op("vector", "tensor_copy", col[:, 0:1], col[:, 1:2],
                      reads=(rcol,), writes=(rcol,))
            else:
                em.op("vector", "tensor_add", col[:, 0:1], col[:, 0:1], col[:, 1:2],
                      reads=(rcol,), writes=(rcol,))
        ps = ss.enter_context(nc.psum_tensor(f"{tag}_ps", [128, 2], F32))
        rps = R(tag, "ps")
        em.op("tensor", "matmul", ps[0:1, 0:1], self.ones_col_f[:, 0:1], col[:, 0:1],
              reads=(R("ones"), rcol), writes=(rps,), start=True, stop=True)
        em.op("vector", "tensor_copy", col[0:1, 1:2], ps[0:1, 0:1],
              reads=(rps,), writes=(rcol,))
        em.op("tensor", "matmul", ps[:, 1:2], self.ones_row_f[0:1, :], col[0:1, 1:2],
              reads=(R("ones"), rcol), writes=(rps,), start=True, stop=True)
        em.op("vector", "tensor_copy", col[:, 0:1], ps[:, 1:2],
              reads=(rps,), writes=(rcol,))
        em.op("vector", "tensor_scalar", col[:, 0:1], col[:, 0:1], 1.0 / (O * I), 1e-5,
              ALU.mult, ALU.max, reads=(rcol,), writes=(rcol,))
        em.op("vector", "reciprocal", self.gcols[:, gi:gi + 1], col[:, 0:1],
              reads=(rcol,), writes=(self.R("gcols"),))
        em.op("vector", "reciprocal", self.gcols[:, gi + 1:gi + 2], self.gcols[:, gi:gi + 1],
              reads=(self.R("gcols"),), writes=(self.R("gcols"),))

    def phase_wprep(self):
        d = self.d
        with ExitStack() as ss:
            mp = self.mp_scope(ss)
            self._bitnet_scale("m1", d["L0_ffn_w1"], HID, D, 0, ss, mp=mp)
            self._bitnet_scale("m2", d["L0_ffn_w2"], D, HID, 2, ss, mp=mp)
            self.mp_flush(mp)
            self.barrier()
        with ExitStack() as ss:
            mp = self.mp_scope(ss)
            self._build_wT("wo", d["L0_attn_Wo"], d["WoT_d"], D, D, None, ss, mp=mp)
            self.mp_flush(mp)
            self.barrier()
        with ExitStack() as ss:
            mp = self.mp_scope(ss)
            self._build_wT("w1", d["L0_ffn_w1"], d["w1qT_d"], HID, D,
                           self.gcols[:, 0:1], ss, pm=True, mp=mp)
            self.mp_flush(mp)
            self.barrier()
        with ExitStack() as ss:
            mp = self.mp_scope(ss)
            self._build_wT("w2", d["L0_ffn_w2"], d["w2qT_d"], D, HID,
                           self.gcols[:, 2:3], ss, mp=mp, st_eng="scalar")
            self.mp_flush(mp)
            self.barrier()
        self.em.flush()
        self.tap("gcols", self.gcols)


    def mm(self, psum_res, out_ap, lhsT_ap, rhs_ap, reads, start, stop, **kw):
        if start:
            return self.em.op("tensor", "matmul", out_ap, lhsT_ap, rhs_ap,
                              reads=tuple(reads), writes=(psum_res,),
                              psum_stop=stop, start=True, stop=stop, **kw)
        return self.em._mm_mid(psum_res, out_ap, lhsT_ap, rhs_ap, tuple(reads), stop, **kw)

    def _load_wT_sbuf(self, tag, src, dst, dst_res, O, I, ss, nbuf=2, stg_ext=None):
        nc, em, R = self.nc, self.em, self.R
        if stg_ext is not None:
            nbuf = 1
        stg = ([stg_ext[0]] if stg_ext is not None else
               [ss.enter_context(nc.sbuf_tensor(f"{tag}_st{k}", [128, I], F32)) for k in range(nbuf)])
        wb = [ss.enter_context(nc.sbuf_tensor(f"{tag}_wb{k}", [128, I], BF16)) for k in range(nbuf)]
        ps = [ss.enter_context(nc.psum_tensor(f"{tag}_ps{k}", [128, 1024], BF16)) for k in range(2)]
        rstg = ([stg_ext[1]] if stg_ext is not None else
                [R(tag, "stg", k) for k in range(nbuf)])
        rwb = [R(tag, "wb", k) for k in range(nbuf)]
        rps = [R(tag, "ps", k) for k in range(2)]
        ti = 0
        for oo in range(O // 128):
            p = oo % nbuf
            self.dma_in(stg[p][:, :], src[oo * 128:(oo + 1) * 128, :], writes=(rstg[p],))
            em.op("vector", "tensor_copy", wb[p][:, :], stg[p][:, :],
                  reads=(rstg[p],), writes=(rwb[p],))
            for ii in range(I // 128):
                pp = ti % 2
                em.op("tensor", "transpose", ps[pp][:, 0:128], wb[p][:, ii * 128:(ii + 1) * 128],
                      self.ident[:, :], reads=(rwb[p], R("ident")), writes=(rps[pp],))
                em.op("scalar", "copy", dst[:, ii * O + oo * 128: ii * O + (oo + 1) * 128],
                      ps[pp][:, 0:128], reads=(rps[pp],), writes=(dst_res,))
                ti += 1

    def emit_rope(self, src, src_res, Aap, Ares, Bap, Bres, xs, xs_res, t1, t1_res,
                  dst_ap, dst_res, W):
        em = self.em
        for b in range(4):
            s, d = 32 * (b ^ 1), 32 * b
            em.op("vector", "tensor_copy", xs[d:d + 32, 0:W], src[s:s + 32, 0:W],
                  reads=(src_res,), writes=(xs_res,))
        em.op("vector", "tensor_tensor", t1[:, 0:W], src[:, 0:W], Aap, ALU.mult,
              reads=(src_res, Ares), writes=(t1_res,))
        em.op("vector", "tensor_tensor", xs[:, 0:W], xs[:, 0:W], Bap, ALU.mult,
              reads=(xs_res, Bres), writes=(xs_res,))
        em.op("vector", "tensor_add", dst_ap, t1[:, 0:W], xs[:, 0:W],
              reads=(t1_res, xs_res), writes=(dst_res,))

    def emit_xn_fm(self, tt0, ntiles, xnt, rxnt, dst, rdst, W, psA, rpsA, psB, rpsB):
        em = self.em
        for j in range(ntiles):
            self.emit_xn_tile(tt0 + j, xnt, rxnt)
            for c in range(DC):
                pr, ps = (rpsA, psA) if (j * DC + c) % 2 == 0 else (rpsB, psB)
                em.op("tensor", "transpose", ps[:, 0:128], xnt[:, c * 128:(c + 1) * 128],
                      self.identf[:, :], reads=(rxnt, self.R("ident")), writes=(pr,))
                em.op("scalar", "copy", dst[:, c * W + j * 128: c * W + (j + 1) * 128],
                      ps[:, 0:128], reads=(pr,), writes=(rdst,))

    def phase_l0(self):
        nc, em, R, d = self.nc, self.em, self.R, self.d
        with ExitStack() as s0:
            K_fm = s0.enter_context(nc.sbuf_tensor("K_fm", [128, DC * T], BF16))
            v_ext = s0.enter_context(nc.sbuf_tensor("v_ext", [128, NH * 16 * 65], BF16))
            cT = s0.enter_context(nc.sbuf_tensor("cT", [128, DC * HALF], BF16))
            carry = s0.enter_context(nc.sbuf_tensor("carry", [128, DC], BF16))
            carry_cx = s0.enter_context(nc.sbuf_tensor("carry_cx", [1, 1], F32))
            rK, rV, rcT = R("K_fm"), R("v_ext"), R("cT")
            rcar, rcarcx = R("carry"), R("carry_cx")

            with ExitStack() as sa:
                xnt = sa.enter_context(nc.sbuf_tensor("a_xnt", [128, D], F32))
                xnq = sa.enter_context(nc.sbuf_tensor("a_xnq", [128, DC * QB], BF16))
                rxnt, rxnq = R("a_xnt"), R("a_xnq")
                psK = [sa.enter_context(nc.psum_tensor(f"a_psK{k}", [128, 512], F32)) for k in range(2)]
                psT = [sa.enter_context(nc.psum_tensor(f"a_psT{k}", [128, 512], F32)) for k in range(2)]
                rpsK = [R("a_psK", k) for k in range(2)]
                rpsT = [R("a_psT", k) for k in range(2)]
                with ExitStack() as s1:
                    mpK = self.mp_scope(s1, nbuf=1)
                    tabA = s1.enter_context(nc.sbuf_tensor("tabA", [128, QB], F32))
                    tabB = s1.enter_context(nc.sbuf_tensor("tabB", [128, QB], F32))
                    wkT = s1.enter_context(nc.sbuf_tensor("wkT", [128, DC * 1024], BF16))
                    kraw = s1.enter_context(nc.sbuf_tensor("a_kraw", [128, QB], F32))
                    xs = s1.enter_context(nc.sbuf_tensor("a_xs", [128, QB], F32))
                    t1 = s1.enter_context(nc.sbuf_tensor("a_t1", [128, QB], F32))
                    bkc = s1.enter_context(nc.sbuf_tensor("a_bkc", [128, DC], F32))
                    rtab, rwkT, rkraw = R("tabs"), R("wkT"), R("a_kraw")
                    rxs, rt1, rbkc = R("a_xs"), R("a_t1"), R("a_bkc")
                    self.dma_in(bkc[:, :], d["bk_p"].rearrange("(c p) -> p c", p=128),
                                writes=(rbkc,), slow=True)
                    self._load_wT_sbuf("wk", d["Wk_p"], wkT, rwkT, D, D, s1)
                    for qb in range(NQB):
                        q0 = qb * QB
                        self.dma_in(tabA[:, :], d["c_ropeA"][:, q0:q0 + QB], writes=(rtab,))
                        self.dma_in(tabB[:, :], d["c_ropeB"][:, q0:q0 + QB], writes=(rtab,))
                        self.emit_xn_fm(qb * 4, 4, xnt, rxnt, xnq, rxnq, QB,
                                        psT[0], rpsT[0], psT[1], rpsT[1])
                        for mc in range(DC):
                            pr, ps = rpsK[mc % 2], psK[mc % 2]
                            for ii in range(DC):
                                self.mm(pr, ps[:, :],
                                        wkT[:, ii * 1024 + mc * 128: ii * 1024 + (mc + 1) * 128],
                                        xnq[:, ii * QB:(ii + 1) * QB],
                                        (rwkT, rxnq), ii == 0, ii == DC - 1)
                            em.op("vector", "tensor_scalar", kraw[:, :], ps[:, :],
                                  bkc[:, mc:mc + 1], None, ALU.add,
                                  reads=(pr, rbkc), writes=(rkraw,))
                            self.emit_rope(kraw, rkraw, tabA[:, :], rtab,
                                           tabB[:, :], rtab, xs, rxs, t1, rt1,
                                           K_fm[:, mc * T + q0: mc * T + q0 + QB], rK, QB)
                            self.mp_step(mpK, 2)
                    self.mp_flush(mpK)
                    self.barrier()
                with ExitStack() as s2:
                    mpV = self.mp_scope(s2, nbuf=1)
                    wvT = s2.enter_context(nc.sbuf_tensor("wvT", [128, DC * 1024], BF16))
                    bvr = s2.enter_context(nc.sbuf_tensor("a_bvr", [1, D], F32))
                    bvb = s2.enter_context(nc.sbuf_tensor("a_bvb", [1, D], BF16))
                    rwvT, rbvb = R("wvT"), R("a_bvb")
                    self.dma_in(bvr[:, :], d["L0_attn_bv"].rearrange("(o d) -> o d", o=1), writes=(R("a_bvr"),))
                    em.op("vector", "tensor_copy", bvb[:, :], bvr[:, :],
                          reads=(R("a_bvr"),), writes=(rbvb,))
                    em.op("vector", "memset", v_ext[:, :], 1.0, writes=(rV,))
                    self._load_wT_sbuf("wv", d["L0_attn_Wv"], wvT, rwvT, D, D, s2)
                    for qb in range(NQB):
                        self.emit_xn_fm(qb * 4, 4, xnt, rxnt, xnq, rxnq, QB,
                                        psT[0], rpsT[0], psT[1], rpsT[1])
                        for j in range(4):
                            for bk in range(2):
                                pr, ps = rpsK[bk], psK[bk]
                                for ii in range(DC):
                                    self.mm(pr, ps[:, :],
                                            xnq[:, ii * QB + j * 128: ii * QB + (j + 1) * 128],
                                            wvT[:, ii * 1024 + bk * 512: ii * 1024 + (bk + 1) * 512],
                                            (rxnq, rwvT), ii == 0, False)
                                self.mm(pr, ps[:, :], self.ones_row_bf[0:1, :],
                                        bvb[0:1, bk * 512:(bk + 1) * 512],
                                        (R("ones"), rbvb), False, True)
                                kc = qb * 4 + j
                                for hh in range(8):
                                    h = bk * 8 + hh
                                    em.op("vector", "tensor_copy",
                                          v_ext[:, (h * 16 + kc) * 65:(h * 16 + kc) * 65 + 64],
                                          ps[:, hh * 64:(hh + 1) * 64],
                                          reads=(pr,), writes=(rV,))
                                self.mp_step(mpV, 2)
                    self.mp_flush(mpV)
                    self.barrier()

                self.barrier()

            self.tap("K_fm", K_fm)
            self.tap("v_ext", v_ext)
            for half in range(2):
                self.phase_B0(half, K_fm, rK, v_ext, rV, cT, rcT)
                self.phase_C0(half, cT, rcT, carry, rcar, carry_cx, rcarcx)
        self.tap("x_l0", self.x)

    def phase_B0(self, half, K_fm, rK, v_ext, rV, cT, rcT):
        nc, em, R, d = self.nc, self.em, self.R, self.d
        with ExitStack() as sb:
            mpB = self.mp_scope(sb, nbuf=1)
            wqT = sb.enter_context(nc.sbuf_tensor("wqT", [128, DC * 1024], BF16))
            xnt = sb.enter_context(nc.sbuf_tensor("b_xnt", [128, D], F32))
            xnq = sb.enter_context(nc.sbuf_tensor("b_xnq", [128, DC * QB], BF16))
            Qq = sb.enter_context(nc.sbuf_tensor("b_Qq", [128, DC * QB], BF16))
            qraw = sb.enter_context(nc.sbuf_tensor("b_qraw", [128, QB], F32))
            xs = sb.enter_context(nc.sbuf_tensor("b_xs", [128, QB], F32))
            t1 = sb.enter_context(nc.sbuf_tensor("b_t1", [128, QB], F32))
            tabA = sb.enter_context(nc.sbuf_tensor("b_tabA", [128, QB], F32))
            tabB = sb.enter_context(nc.sbuf_tensor("b_tabB", [128, QB], F32))
            bqc = sb.enter_context(nc.sbuf_tensor("b_bqc", [128, DC], F32))
            pT = [sb.enter_context(nc.sbuf_tensor(f"b_pT{k}", [128, QB], BF16)) for k in range(2)]
            zT = sb.enter_context(nc.sbuf_tensor("b_zT", [128, 8], F32))
            rcp = sb.enter_context(nc.sbuf_tensor("b_rcp", [128, 8], F32))
            zsb = qraw
            rzs = xs
            rzb = t1
            psS = [sb.enter_context(nc.psum_tensor(f"b_psS{k}", [128, 512], F32)) for k in range(2)]
            psO = [sb.enter_context(nc.psum_tensor(f"b_psO{k}", [128, 512], F32)) for k in range(2)]
            psB = sb.enter_context(nc.psum_tensor("b_psB", [128, 512], F32))
            rwqT, rxnt, rxnq, rQq, rqraw = R("wqT"), R("b_xnt"), R("b_xnq"), R("b_Qq"), R("b_qraw")
            rxs, rt1, rtab, rbqc = R("b_xs"), R("b_t1"), R("b_tab"), R("b_bqc")
            rpT = [R("b_pT", k) for k in range(2)]
            rzsb, rzT, rrcp, rrzs, rrzb = R("b_qraw"), R("b_zT"), R("b_rcp"), R("b_xs"), R("b_t1")
            rpsS = [R("b_psS", k) for k in range(2)]
            rpsO = [R("b_psO", k) for k in range(2)]
            rpsB = R("b_psB")

            self.dma_in(bqc[:, :], d["bq_p"].rearrange("(c p) -> p c", p=128),
                        writes=(rbqc,), slow=True)
            em.op("vector", "tensor_scalar", bqc[:, :], bqc[:, :], 0.125, None, ALU.mult,
                  reads=(rbqc,), writes=(rbqc,))
            self._load_wT_sbuf("wq", d["Wq_p"], wqT, rwqT, D, D, sb,
                               stg_ext=(mpB["stg"][0], mpB["rstg"][0]))
            for ql in range(2):
                qb = half * 2 + ql
                q0 = qb * QB
                self.dma_in(tabA[:, :], d["c_ropeA"][:, q0:q0 + QB], writes=(rtab,))
                self.dma_in(tabB[:, :], d["c_ropeB"][:, q0:q0 + QB], writes=(rtab,))
                self.emit_xn_fm(qb * 4, 4, xnt, rxnt, xnq, rxnq, QB,
                                psS[0], rpsS[0], psS[1], rpsS[1])
                for mc in range(DC):
                    pr, ps = rpsS[mc % 2], psS[mc % 2]
                    for ii in range(DC):
                        self.mm(pr, ps[:, :],
                                wqT[:, ii * 1024 + mc * 128: ii * 1024 + (mc + 1) * 128],
                                xnq[:, ii * QB:(ii + 1) * QB],
                                (rwqT, rxnq), ii == 0, ii == DC - 1)
                    em.op("vector", "tensor_scalar", qraw[:, :], ps[:, :],
                          0.125, bqc[:, mc:mc + 1], ALU.mult, ALU.add,
                          reads=(pr, rbqc), writes=(rqraw,))
                    self.emit_rope(qraw, rqraw, tabA[:, :], rtab, tabB[:, :], rtab,
                                   xs, rxs, t1, rt1,
                                   Qq[:, mc * QB:(mc + 1) * QB], rQq, QB)
                nkc = qb * 4 + 4
                for h in range(NH):
                    hb = 64 * (h % 2)
                    po = h % 2
                    for kc in range(nkc):
                        ss = kc % 2
                        self.mm(rpsS[ss], psS[ss][:, :],
                                K_fm[hb:hb + 64, (h // 2) * T + kc * 128:(h // 2) * T + (kc + 1) * 128],
                                Qq[hb:hb + 64, (h // 2) * QB:(h // 2 + 1) * QB],
                                (rK, rQq), True, True)
                        em.op("scalar", "activation", pT[ss][:, :], psS[ss][:, :], ACT.Exp,
                              reads=(rpsS[ss],), writes=(rpT[ss],))
                        j = kc - qb * 4
                        if j >= 0:
                            em.op("vector", "tensor_mul", pT[ss][:, :], pT[ss][:, :],
                                  self.masks[:, j * 512:(j + 1) * 512],
                                  reads=(rpT[ss], R("masks")), writes=(rpT[ss],))
                        self.mm(rpsO[po], psO[po][0:65, 0:512],
                                v_ext[:, (h * 16 + kc) * 65:(h * 16 + kc + 1) * 65],
                                pT[ss][:, :], (rV, rpT[ss]), kc == 0, kc == nkc - 1)
                    if h % 2 == 1:
                        for i in range(2):
                            em.op("vector", "tensor_copy", zsb[32 * i:32 * i + 1, 0:512],
                                  psO[i][64:65, 0:512], reads=(rpsO[i],), writes=(rzsb,))
                        for i in range(2):
                            for s in range(4):
                                em.op("tensor", "transpose", psB[:, 0:1],
                                      zsb[32 * i:32 * i + 1, s * 128:(s + 1) * 128],
                                      self.identf[32 * i:32 * i + 1, 32 * i:32 * i + 1],
                                      reads=(rzsb, R("ident")), writes=(rpsB,))
                                em.op("vector", "tensor_copy", zT[:, i * 4 + s:i * 4 + s + 1],
                                      psB[:, 0:1], reads=(rpsB,), writes=(rzT,))
                        em.op("vector", "reciprocal", rcp[:, :], zT[:, :],
                              reads=(rzT,), writes=(rrcp,))
                        for i in range(2):
                            for s in range(4):
                                em.op("tensor", "transpose", psB[0:1, 0:128],
                                      rcp[:, i * 4 + s:i * 4 + s + 1], self.identf[:, :],
                                      reads=(rrcp, R("ident")), writes=(rpsB,))
                                em.op("vector", "tensor_copy",
                                      rzs[32 * i:32 * i + 1, s * 128:(s + 1) * 128],
                                      psB[0:1, 0:128], reads=(rpsB,), writes=(rrzs,))
                        for i in range(2):
                            hh = h - 1 + i
                            self.mm(rpsB, psB[:, :], self.ones_row_f[32 * i:32 * i + 1, :],
                                    rzs[32 * i:32 * i + 1, 0:512], (R("ones"), rrzs), True, True)
                            em.op("vector", "tensor_copy", rzb[:, :], psB[:, :],
                                  reads=(rpsB,), writes=(rrzb,))
                            em.op("vector", "tensor_mul",
                                  cT[64 * (hh % 2):64 * (hh % 2) + 64,
                                     (hh // 2) * HALF + ql * QB:(hh // 2) * HALF + (ql + 1) * QB],
                                  psO[i][0:64, 0:512], rzb[0:64, :],
                                  reads=(rpsO[i], rrzb), writes=(rcT,))
                    self.mp_step(mpB, 4)
            self.mp_flush(mpB)
            self.barrier()
        self.tap(f"cT{half}", cT)

    def phase_C0(self, half, cT, rcT, carry, rcar, carry_cx, rcarcx):
        nc, em, R, d = self.nc, self.em, self.R, self.d
        h0 = half * HALF
        with ExitStack() as sc:
            xqf = sc.enter_context(nc.sbuf_tensor("c_xqf", [128, DC * 1026], BF16))
            cxr = sc.enter_context(nc.sbuf_tensor("c_cxr", [1, 1026], F32))
            rxqf, rcxr = R("c_xqf"), R("c_cxr")
            with ExitStack() as sp:
                xnt2 = [sp.enter_context(nc.sbuf_tensor(f"c_xnt{k}", [128, D], F32)) for k in range(2)]
                vb2 = [sp.enter_context(nc.sbuf_tensor(f"c_vb{k}", [128, D], F32)) for k in range(2)]
                xqt2 = [sp.enter_context(nc.sbuf_tensor(f"c_xqt{k}", [128, D], BF16)) for k in range(2)]
                scol2 = [sp.enter_context(nc.sbuf_tensor(f"c_scol{k}", [128, 2], F32)) for k in range(2)]
                psT = [sp.enter_context(nc.psum_tensor(f"c_psT{k}", [128, 1024], BF16)) for k in range(2)]
                psF = sp.enter_context(nc.psum_tensor("c_psF", [128, 512], F32))
                rxnt2 = [R("c_xnt", k) for k in range(2)]
                rvb2 = [R("c_vb", k) for k in range(2)]
                rxqt2 = [R("c_xqt", k) for k in range(2)]
                rscol2 = [R("c_scol", k) for k in range(2)]
                rpsT = [R("c_psT", k) for k in range(2)]
                rpsF = R("c_psF")
                ntl = 9 if half == 0 else 8
                for tl in range(ntl):
                    tt = half * 8 + tl
                    edge = (tl == 8)
                    xnt, vb, xqt, scol = xnt2[tl % 2], vb2[tl % 2], xqt2[tl % 2], scol2[tl % 2]
                    rxnt, rvb, rxqt, rscol = rxnt2[tl % 2], rvb2[tl % 2], rxqt2[tl % 2], rscol2[tl % 2]
                    self.emit_xn_tile(tt, xnt, rxnt)
                    em.op("vector", "reduce_max", scol[:, 0:1], xnt[:, :], AX.X,
                          apply_absolute_value=True, reads=(rxnt,), writes=(rscol,))
                    em.op("vector", "tensor_scalar", scol[:, 0:1], scol[:, 0:1], 1e-5,
                          None, ALU.max, reads=(rscol,), writes=(rscol,))
                    em.op("vector", "reciprocal", scol[:, 1:2], scol[:, 0:1],
                          reads=(rscol,), writes=(rscol,))
                    em.op("vector", "tensor_scalar", scol[:, 1:2], scol[:, 1:2], 127.0,
                          None, ALU.mult, reads=(rscol,), writes=(rscol,))
                    em.op("vector", "tensor_scalar", vb[:, :], xnt[:, :], scol[:, 1:2],
                          None, ALU.mult, reads=(rxnt, rscol), writes=(rvb,))
                    em.op("vector", "tensor_scalar", xqt[:, :], vb[:, :], 12582912.0,
                          12582912.0, ALU.add, ALU.subtract, reads=(rvb,), writes=(rxqt,))
                    em.op("vector", "reciprocal", self.cxcols[:, tt % NTT:tt % NTT + 1],
                          scol[:, 1:2], reads=(rscol,), writes=(R("cxcols"),))
                    for c in range(DC):
                        pp = c % 2
                        em.op("tensor", "transpose", psT[pp][:, 0:128],
                              xqt[:, c * 128:(c + 1) * 128], self.ident[:, :],
                              reads=(rxqt, R("ident")), writes=(rpsT[pp],))
                        if edge:
                            em.op("scalar", "copy", xqf[:, c * 1026 + 1025: c * 1026 + 1026],
                                  psT[pp][:, 0:1], reads=(rpsT[pp],), writes=(rxqf,))
                        else:
                            em.op("scalar", "copy",
                                  xqf[:, c * 1026 + 1 + tl * 128: c * 1026 + 1 + (tl + 1) * 128],
                                  psT[pp][:, 0:128], reads=(rpsT[pp],), writes=(rxqf,))
                    em.op("tensor", "transpose", psF[0:1, 0:128],
                          self.cxcols[:, tt % NTT:tt % NTT + 1], self.identf[:, :],
                          reads=(R("cxcols"), R("ident")), writes=(rpsF,))
                    if edge:
                        em.op("vector", "tensor_copy", cxr[0:1, 1025:1026],
                              psF[0:1, 0:1], reads=(rpsF,), writes=(rcxr,))
                    else:
                        em.op("vector", "tensor_copy", cxr[0:1, 1 + tl * 128: 1 + (tl + 1) * 128],
                              psF[0:1, 0:128], reads=(rpsF,), writes=(rcxr,))
                if half == 0:
                    em.op("vector", "memset",
                          xqf[:, :].rearrange("p (c w) -> p c w", w=1026)[:, :, 0:1], 0.0,
                          writes=(rxqf,))
                    em.op("vector", "memset", cxr[0:1, 0:1], 0.0, writes=(rcxr,))
                    em.op("vector", "tensor_copy", carry[:, :],
                          xqf[:, :].rearrange("p (c w) -> p c w", w=1026)[:, :, 1024:1025].rearrange("p c w -> p (c w)"),
                          reads=(rxqf,), writes=(rcar,))
                    em.op("vector", "tensor_copy", carry_cx[0:1, :], cxr[0:1, 1024:1025],
                          reads=(rcxr,), writes=(rcarcx,))
                else:
                    em.op("vector", "tensor_copy",
                          xqf[:, :].rearrange("p (c w) -> p c w", w=1026)[:, :, 0:1].rearrange("p c w -> p (c w)"),
                          carry[:, :], reads=(rcar,), writes=(rxqf,))
                    em.op("vector", "tensor_copy", cxr[0:1, 0:1], carry_cx[0:1, :],
                          reads=(rcarcx,), writes=(rcxr,))
                    em.op("vector", "memset",
                          xqf[:, :].rearrange("p (c w) -> p c w", w=1026)[:, :, 1025:1026], 0.0,
                          writes=(rxqf,))
                    em.op("vector", "memset", cxr[0:1, 1025:1026], 0.0, writes=(rcxr,))
                self.barrier()
            with ExitStack() as sm:
                w1m = [sm.enter_context(nc.sbuf_tensor(f"c_w1m{k}", [128, DC * 128], BF16)) for k in range(2)]
                w2m = [sm.enter_context(nc.sbuf_tensor(f"c_w2m{k}", [128, 1024], BF16)) for k in range(2)]
                wot = [sm.enter_context(nc.sbuf_tensor(f"c_wot{k}", [128, 1024], BF16)) for k in range(2)]
                cwc = sm.enter_context(nc.sbuf_tensor("c_cwc", [128, HC * 3], F32))
                cbc = sm.enter_context(nc.sbuf_tensor("c_cbc", [128, HC], F32))
                bob = sm.enter_context(nc.sbuf_tensor("c_bob", [1, D], BF16))
                cxb = sm.enter_context(nc.sbuf_tensor("c_cxb", [128, 258], F32))
                hx = sm.enter_context(nc.sbuf_tensor("c_hx", [128, 258], F32))
                cva = sm.enter_context(nc.sbuf_tensor("c_cva", [128, 512], F32))
                h2 = sm.enter_context(nc.sbuf_tensor("c_h2", [128, HC * 256], BF16))
                am = sm.enter_context(nc.sbuf_tensor("c_am", [128, 256], F32))
                s127b = sm.enter_context(nc.sbuf_tensor("c_s127b", [128, 256], F32))
                c2c = sm.enter_context(nc.sbuf_tensor("c_c2c", [128, 2], F32))
                amx = sm.enter_context(nc.sbuf_tensor("c_amx", [128, 4], F32))
                xq2 = [sm.enter_context(nc.sbuf_tensor(f"c_xq2{k}", [128, 128], BF16)) for k in range(2)]
                gq = cva
                psH = [sm.enter_context(nc.psum_tensor(f"c_psH{k}", [128, 512], F32)) for k in range(2)]
                psY = [sm.enter_context(nc.psum_tensor(f"c_psY{k}", [128, 512], F32)) for k in range(4)]
                psSm = [sm.enter_context(nc.psum_tensor(f"c_psS{k}", [128, 512], F32)) for k in range(2)]
                rw1m = [R("c_w1m", k) for k in range(2)]
                rw2m = [R("c_w2m", k) for k in range(2)]
                rwot = [R("c_wot", k) for k in range(2)]
                rcw, rcb, rbob = R("c_cwc"), R("c_cbc"), R("c_bob")
                rcxb, rhx, rcva, rh2 = R("c_cxb"), R("c_hx"), R("c_cva"), R("c_h2")
                ram, rs127b, rc2c = R("c_am"), R("c_s127b"), R("c_c2c")
                ramx = R("c_amx")
                rxq2 = [R("c_xq2", k) for k in range(2)]
                rgq = R("c_cva")
                rpsH = [R("c_psH", k) for k in range(2)]
                rpsY = [R("c_psY", k) for k in range(4)]
                rpsSm = [R("c_psS", k) for k in range(2)]

                self.dma_in(cwc[:, :].rearrange("p (c k) -> p c k", k=3),
                            d["L0_ffn_conv_w"].rearrange("(c p) one k -> p c (one k)", p=128),
                            writes=(rcw,), slow=True)
                self.dma_in(cbc[:, :], d["L0_ffn_conv_b"].rearrange("(c p) -> p c", p=128),
                            writes=(rcb,), slow=True)
                for c2 in range(2):
                    self.dma_in(gq[0:1, 0:512],
                                d["L0_attn_bo"].rearrange("(o d) -> o d", o=1)[:, c2 * 512:(c2 + 1) * 512],
                                writes=(rgq,))
                    em.op("vector", "tensor_copy", bob[0:1, c2 * 512:(c2 + 1) * 512],
                          gq[0:1, 0:512], reads=(rgq,), writes=(rbob,))
                for grp in range(4):
                    tl0 = grp * 256
                    self.mm(rpsH[0], psH[0][:, 0:258], self.ones_row_f[0:1, :],
                            cxr[0:1, tl0:tl0 + 258], (R("ones"), rcxr), True, True)
                    em.op("vector", "tensor_scalar", cxb[:, :], psH[0][:, 0:258],
                          self.gcols[:, 1:2], None, ALU.mult,
                          reads=(rpsH[0], R("gcols")), writes=(rcxb,))
                    for m in range(HC):
                        p = m % 2
                        self.dma_in(w1m[p][:, :],
                                    d["w1qT_d"][:, m * DC * 128:(m + 1) * DC * 128],
                                    writes=(rw1m[p],), chan=self.cW[p],
                                    extra_waits=tuple(R("w1", "dram").wt.items()))
                        ph = m % 2
                        for dd in range(DC):
                            self.mm(rpsH[ph], psH[ph][:, 0:258],
                                    w1m[p][:, dd * 128:(dd + 1) * 128],
                                    xqf[:, dd * 1026 + tl0: dd * 1026 + tl0 + 258],
                                    (rw1m[p], rxqf), dd == 0, dd == DC - 1)
                        em.op("vector", "tensor_tensor", hx[:, :], psH[ph][:, 0:258],
                              cxb[:, :], ALU.mult,
                              reads=(rpsH[ph], rcxb), writes=(rhx,))
                        em.op("vector", "tensor_scalar", cva[:, 0:256], hx[:, 0:256],
                              cwc[:, m * 3:m * 3 + 1], cbc[:, m:m + 1], ALU.mult, ALU.add,
                              reads=(rhx, rcw, rcb), writes=(rcva,))
                        em.op("vector", "scalar_tensor_tensor", cva[:, 0:256], hx[:, 1:257],
                              cwc[:, m * 3 + 1:m * 3 + 2], cva[:, 0:256], ALU.mult, ALU.add,
                              reads=(rhx, rcw, rcva), writes=(rcva,))
                        em.op("vector", "scalar_tensor_tensor", cva[:, 0:256], hx[:, 2:258],
                              cwc[:, m * 3 + 2:m * 3 + 3], cva[:, 0:256], ALU.mult, ALU.add,
                              reads=(rhx, rcw, rcva), writes=(rcva,))
                        em.op("scalar", "activation", h2[:, m * 256:(m + 1) * 256],
                              cva[:, 0:256], ACT.Silu, reads=(rcva,), writes=(rh2,))
                        mp_ = m - 1
                        if mp_ == 0:
                            em.op("vector", "scalar_tensor_tensor", am[:, :],
                                  h2[:, 0:256], -1.0, h2[:, 0:256], ALU.mult, ALU.max,
                                  reads=(rh2,), writes=(ram,))
                        elif mp_ > 0:
                            em.op("vector", "scalar_tensor_tensor", am[:, :],
                                  h2[:, mp_ * 256:(mp_ + 1) * 256], -1.0, am[:, :],
                                  ALU.mult, ALU.max, reads=(rh2, ram), writes=(ram,))
                        if mp_ >= 1:
                            em.op("vector", "tensor_tensor", am[:, :], am[:, :],
                                  h2[:, mp_ * 256:(mp_ + 1) * 256], ALU.max,
                                  reads=(ram, rh2), writes=(ram,))
                        if m == HC - 1:
                            em.op("vector", "scalar_tensor_tensor", am[:, :],
                                  h2[:, m * 256:(m + 1) * 256], -1.0, am[:, :],
                                  ALU.mult, ALU.max, reads=(rh2, ram), writes=(ram,))
                            em.op("vector", "tensor_tensor", am[:, :], am[:, :],
                                  h2[:, m * 256:(m + 1) * 256], ALU.max,
                                  reads=(ram, rh2), writes=(ram,))
                    for s2 in range(2):
                        em.op("tensor", "transpose", psH[s2][:, 0:128],
                              am[:, s2 * 128:(s2 + 1) * 128],
                              self.identf[:, :], reads=(ram, R("ident")), writes=(rpsH[s2],))
                        em.op("vector", "tensor_copy", cva[:, s2 * 128:(s2 + 1) * 128],
                              psH[s2][:, 0:128], reads=(rpsH[s2],), writes=(rcva,))
                        em.op("vector", "reduce_max", amx[:, s2:s2 + 1],
                              cva[:, s2 * 128:(s2 + 1) * 128], AX.X,
                              reads=(rcva,), writes=(ramx,))
                    em.op("vector", "tensor_scalar", amx[:, 0:2], amx[:, 0:2], 1e-5,
                          None, ALU.max, reads=(ramx,), writes=(ramx,))
                    em.op("vector", "tensor_scalar", c2c[:, 0:2], amx[:, 0:2],
                          self.gcols[:, 3:4], 1.0 / 127.0, ALU.mult, ALU.mult,
                          reads=(ramx, R("gcols")), writes=(rc2c,))
                    em.op("vector", "reciprocal", amx[:, 2:4], amx[:, 0:2],
                          reads=(ramx,), writes=(ramx,))
                    em.op("vector", "tensor_scalar", amx[:, 2:4], amx[:, 2:4], 127.0,
                          None, ALU.mult, reads=(ramx,), writes=(ramx,))
                    for s2 in range(2):
                        em.op("tensor", "transpose", psH[s2][0:1, 128:256],
                              amx[:, 2 + s2:3 + s2],
                              self.identf[:, :], reads=(ramx, R("ident")), writes=(rpsH[s2],))
                        em.op("vector", "tensor_copy", s127b[0:1, s2 * 128:(s2 + 1) * 128],
                              psH[s2][0:1, 128:256], reads=(rpsH[s2],), writes=(rs127b,))
                    self.mm(rpsH[0], psH[0][:, 256:512], self.ones_row_f[0:1, :],
                            s127b[0:1, 0:256], (R("ones"), rs127b), True, True)
                    em.op("vector", "tensor_copy", s127b[:, :], psH[0][:, 256:512],
                          reads=(rpsH[0],), writes=(rs127b,))
                    for m in range(HC):
                        p = m % 2
                        self.dma_in(w2m[p][:, :], d["w2qT_d"][m * 128:(m + 1) * 128, :],
                                    writes=(rw2m[p],), chan=self.cW[2 + p],
                                    extra_waits=tuple(R("w2", "dram").wt.items()))
                        for fq01 in range(2):
                            q = fq01
                            em.op("vector", "tensor_tensor", hx[:, 0:128],
                                  h2[:, m * 256 + fq01 * 128: m * 256 + (fq01 + 1) * 128],
                                  s127b[:, fq01 * 128:(fq01 + 1) * 128], ALU.mult,
                                  reads=(rh2, rs127b), writes=(rhx,))
                            em.op("vector", "tensor_scalar", xq2[q][:, :], hx[:, 0:128],
                                  12582912.0, 12582912.0, ALU.add, ALU.subtract,
                                  reads=(rhx,), writes=(rxq2[q],))
                            for dh in range(2):
                                self.mm(rpsY[fq01 * 2 + dh], psY[fq01 * 2 + dh][:, :],
                                        xq2[q][:, :],
                                        w2m[p][:, dh * 512:(dh + 1) * 512],
                                        (rxq2[q], rw2m[p]), m == 0, m == HC - 1)
                    for fq01 in range(2):
                        tt = half * 8 + grp * 2 + fq01
                        for vf in range(DC):
                            p = vf % 2
                            self.dma_in(wot[p][:, :], d["WoT_d"][vf * 128:(vf + 1) * 128, :],
                                        writes=(rwot[p],), chan=self.cW[4 + p],
                                        extra_waits=tuple(R("wo", "dram").wt.items()))
                            for dh in range(2):
                                self.mm(rpsSm[dh], psSm[dh][:, :],
                                        cT[:, vf * HALF + tl0 + fq01 * 128: vf * HALF + tl0 + fq01 * 128 + 128],
                                        wot[p][:, dh * 512:(dh + 1) * 512],
                                        (rcT, rwot[p]), vf == 0, False)
                        for dh in range(2):
                            self.mm(rpsSm[dh], psSm[dh][:, :], self.ones_row_bf[0:1, :],
                                    bob[0:1, dh * 512:(dh + 1) * 512],
                                    (R("ones"), rbob), False, True)
                        for dh in range(2):
                            em.op("scalar", "activation", gq[:, 0:512],
                                  psY[fq01 * 2 + dh][:, :], ACT.Silu, scale=c2c[:, fq01:fq01 + 1],
                                  reads=(rpsY[fq01 * 2 + dh], rc2c), writes=(rgq,))
                            em.op("vector", "tensor_mul", gq[:, 0:512],
                                  psSm[dh][:, :], gq[:, 0:512],
                                  reads=(rpsSm[dh], rgq), writes=(rgq,))
                            em.op("vector", "tensor_add",
                                  self.x[:, tt * D + dh * 512: tt * D + (dh + 1) * 512],
                                  self.x[:, tt * D + dh * 512: tt * D + (dh + 1) * 512],
                                  gq[:, 0:512], reads=(rgq, R("x")), writes=(R("x"),))
                self.barrier()

    def _xn_fm_range(self, tt0, ntiles, xnt, rxnt, dst, rdst, W, psA, rpsA, psB, rpsB):
        self.emit_xn_fm(tt0, ntiles, xnt, rxnt, dst, rdst, W, psA, rpsA, psB, rpsB)

    def phase_l1(self):
        nc, em, R, d = self.nc, self.em, self.R, self.d
        with ExitStack() as s1:
            acc = s1.enter_context(nc.sbuf_tensor("acc", [128, NTT * D], F32))
            we = s1.enter_context(nc.sbuf_tensor("we", [128, NTT * NE], F32))
            racc, rwe = R("acc"), R("we")

            with ExitStack() as sr:
                xn2 = sr.enter_context(nc.sbuf_tensor("r_xn2", [128, DC * HALF], BF16))
                wef = sr.enter_context(nc.sbuf_tensor("r_wef", [8, T], BF16))
                stg = [sr.enter_context(nc.sbuf_tensor(f"r_stg{k}", [128, DC * 128], F32)) for k in range(2)]
                w1b = [sr.enter_context(nc.sbuf_tensor(f"r_w1b{k}", [128, DC * 128], BF16)) for k in range(2)]
                rw2s = sr.enter_context(nc.sbuf_tensor("r_rw2s", [128, HC * NE], F32))
                rw2b = sr.enter_context(nc.sbuf_tensor("r_rw2b", [128, HC * NE], BF16))
                rb1c = sr.enter_context(nc.sbuf_tensor("r_rb1c", [128, HC], F32))
                rb2c = sr.enter_context(nc.sbuf_tensor("r_rb2c", [8, 1], F32))
                tb = [sr.enter_context(nc.sbuf_tensor(f"r_tb{k}", [128, QB], BF16)) for k in range(2)]
                lg = sr.enter_context(nc.sbuf_tensor("r_lg", [8, QB], F32))
                lgt = sr.enter_context(nc.sbuf_tensor("r_lgt", [128, NE], F32))
                l2v = sr.enter_context(nc.sbuf_tensor("r_l2v", [128, NE], F32))
                eq1 = sr.enter_context(nc.sbuf_tensor("r_eq1", [128, NE], F32))
                mm1c = sr.enter_context(nc.sbuf_tensor("r_m1", [128, 4], F32))
                b2b = sr.enter_context(nc.sbuf_tensor("r_b2b", [8, D], BF16))
                b2s = sr.enter_context(nc.sbuf_tensor("r_b2s", [8, D], F32))
                xnt = sr.enter_context(nc.sbuf_tensor("r_xnt", [128, D], F32))
                psT = [sr.enter_context(nc.psum_tensor(f"r_psT{k}", [128, 512], F32)) for k in range(2)]
                psL = [sr.enter_context(nc.psum_tensor(f"r_psL{k}", [128, 512], F32)) for k in range(2)]
                rxn2, rwef = R("r_xn2"), R("r_wef")
                rstg = [R("r_stg", k) for k in range(2)]
                rw1bb = [R("r_w1b", k) for k in range(2)]
                rrw2b, rrb1, rrb2 = R("r_rw2b"), R("r_rb1c"), R("r_rb2c")
                rtb = [R("r_tb", k) for k in range(2)]
                rlg, rlgt, rl2v, req1, rm1 = (R("r_lg"), R("r_lgt"),
                                              R("r_l2v"), R("r_eq1"), R("r_m1"))
                rb2b, rxnt = R("r_b2b"), R("r_xnt")
                rpsT = [R("r_psT", k) for k in range(2)]
                rpsL = [R("r_psL", k) for k in range(2)]

                mpR = self.mp_scope(sr)
                while self._mp_i < len(self._mp):
                    self.mp_step(mpR, 8)
                self.mp_flush(mpR)
                self.dma_in(rw2s[:, :].rearrange("p (c e) -> p c e", e=NE),
                            d["L1_ffn_rw2"].rearrange("(c p) e -> p c e", p=128),
                            writes=(R("r_rw2s"),), slow=True)
                em.op("vector", "tensor_copy", rw2b[:, :], rw2s[:, :],
                      reads=(R("r_rw2s"),), writes=(rrw2b,))
                self.dma_in(rb1c[:, :], d["L1_ffn_rb1"].rearrange("(c p) -> p c", p=128),
                            writes=(rrb1,), slow=True)
                self.dma_in(rb2c[:, :], d["L1_ffn_rb2"].rearrange("(e o) -> e o", o=1),
                            writes=(rrb2,), slow=True)
                self.dma_in(b2s[:, :], d["L1_ffn_b2"][:, :], writes=(R("r_b2s"),))
                em.op("vector", "tensor_copy", b2b[:, :], b2s[:, :],
                      reads=(R("r_b2s"),), writes=(rb2b,))
                for half in range(2):
                    self.emit_xn_fm(half * 8, 8, xnt, rxnt, xn2, rxn2, HALF,
                                    psT[0], rpsT[0], psT[1], rpsT[1])
                    for hc in range(HC):
                        p = hc % 2
                        self.dma_in(stg[p][:, :], d["rw1_p"][:, hc * DC * 128:(hc + 1) * DC * 128],
                                    writes=(rstg[p],), chan=self.cW[p])
                        em.op("vector", "tensor_copy", w1b[p][:, :], stg[p][:, :],
                              reads=(rstg[p],), writes=(rw1bb[p],))
                        for ql in range(2):
                            for dd in range(DC):
                                self.mm(rpsT[ql], psT[ql][:, :],
                                        w1b[p][:, dd * 128:(dd + 1) * 128],
                                        xn2[:, dd * HALF + ql * 512: dd * HALF + (ql + 1) * 512],
                                        (rw1bb[p], rxn2), dd == 0, dd == DC - 1)
                            em.op("scalar", "activation", tb[ql][:, :], psT[ql][:, :], ACT.Silu,
                                  bias=rb1c[:, hc:hc + 1], reads=(rpsT[ql], rrb1), writes=(rtb[ql],))
                            self.mm(rpsL[ql], psL[ql][0:8, :], rw2b[:, hc * NE:(hc + 1) * NE],
                                    tb[ql][:, :], (rrw2b, rtb[ql]), hc == 0, hc == HC - 1)
                    for ql in range(2):
                        qb = half * 2 + ql
                        em.op("vector", "tensor_scalar", lg[:, :], psL[ql][0:8, :],
                              rb2c[:, 0:1], None, ALU.add, reads=(rpsL[ql], rrb2), writes=(rlg,))
                        for tl in range(4):
                            tt = qb * 4 + tl
                            em.op("tensor", "transpose", psL[ql][0:128, 8:16],
                                  lg[0:8, tl * 128:(tl + 1) * 128], self.identf[0:8, 0:8],
                                  reads=(rlg, R("ident")), writes=(rpsL[ql],))
                            em.op("vector", "tensor_copy", lgt[:, :], psL[ql][:, 8:16],
                                  reads=(rpsL[ql],), writes=(rlgt,))
                            em.op("vector", "reduce_max", mm1c[:, 0:1], lgt[:, :], AX.X,
                                  reads=(rlgt,), writes=(rm1,))
                            em.op("vector", "tensor_scalar", eq1[:, :], lgt[:, :],
                                  mm1c[:, 0:1], None, ALU.is_equal,
                                  reads=(rlgt, rm1), writes=(req1,))
                            em.op("vector", "scalar_tensor_tensor", l2v[:, :], eq1[:, :],
                                  NEG_TIE, lgt[:, :], ALU.mult, ALU.add,
                                  reads=(req1, rlgt), writes=(rl2v,))
                            em.op("vector", "reduce_max", mm1c[:, 1:2], l2v[:, :], AX.X,
                                  reads=(rl2v,), writes=(rm1,))
                            em.op("vector", "tensor_sub", mm1c[:, 2:3], mm1c[:, 0:1],
                                  mm1c[:, 1:2], reads=(rm1,), writes=(rm1,))
                            em.op("scalar", "activation", mm1c[:, 2:3], mm1c[:, 2:3],
                                  ACT.Sigmoid, reads=(rm1,), writes=(rm1,))
                            em.op("vector", "tensor_scalar", mm1c[:, 3:4], mm1c[:, 2:3],
                                  -1.0, 1.0, ALU.mult, ALU.add, reads=(rm1,), writes=(rm1,))
                            em.op("vector", "tensor_scalar", eq1[:, :], eq1[:, :],
                                  mm1c[:, 2:3], None, ALU.mult, reads=(req1, rm1), writes=(req1,))
                            em.op("vector", "tensor_scalar", l2v[:, :], l2v[:, :],
                                  mm1c[:, 1:2], None, ALU.is_equal,
                                  reads=(rl2v, rm1), writes=(rl2v,))
                            em.op("vector", "scalar_tensor_tensor", we[:, tt * NE:(tt + 1) * NE],
                                  l2v[:, :], mm1c[:, 3:4], eq1[:, :], ALU.mult, ALU.add,
                                  reads=(rl2v, rm1, req1), writes=(rwe,))
                            em.op("tensor", "transpose", psL[ql][0:8, 16:144],
                                  we[:, tt * NE:(tt + 1) * NE], self.identf[:, :],
                                  reads=(rwe, R("ident")), writes=(rpsL[ql],))
                            em.op("vector", "tensor_copy", wef[0:8, tt * 128:(tt + 1) * 128],
                                  psL[ql][0:8, 16:144], reads=(rpsL[ql],), writes=(rwef,))
                for tt in range(NTT):
                    for dh in range(2):
                        pp = (tt * 2 + dh) % 2
                        self.mm(rpsT[pp], psT[pp][:, :],
                                wef[0:8, tt * 128:(tt + 1) * 128],
                                b2b[0:8, dh * 512:(dh + 1) * 512],
                                (rwef, rb2b), True, True)
                        em.op("vector", "tensor_copy",
                              acc[:, tt * D + dh * 512: tt * D + (dh + 1) * 512],
                              psT[pp][:, :], reads=(rpsT[pp],), writes=(racc,))
                self.barrier()

            with ExitStack() as sm:
                xn2 = sm.enter_context(nc.sbuf_tensor("m_xn2", [128, DC * HALF], BF16))
                hwin = sm.enter_context(nc.sbuf_tensor("m_hwin", [128, 16 * HALF // 1], BF16)) \
                    if False else sm.enter_context(nc.sbuf_tensor("m_hwin", [128, 16 * 1024], BF16))
                w1b = [sm.enter_context(nc.sbuf_tensor(f"m_w1b{k}", [128, DC * 128], BF16)) for k in range(2)]
                w2b = [sm.enter_context(nc.sbuf_tensor(f"m_w2b{k}", [128, D], BF16)) for k in range(4)]
                b1c = sm.enter_context(nc.sbuf_tensor("m_b1c", [128, HC], F32))
                xnt = sm.enter_context(nc.sbuf_tensor("m_xnt", [128, D], F32))
                psY = [sm.enter_context(nc.psum_tensor(f"m_psY{k}", [128, 512], F32)) for k in range(8)]
                rxn2, rhwin, rb1 = (R("m_xn2"), R("m_hwin"), R("m_b1c"))
                rw1b = [R("m_w1b", k) for k in range(2)]
                rw2b2 = [R("m_w2b", k) for k in range(4)]
                rxnt = R("m_xnt")
                rpsY = [R("m_psY", k) for k in range(8)]
                for half in range(2):
                    self.emit_xn_fm(half * 8, 8, xnt, rxnt, xn2, rxn2, HALF,
                                    psY[0], rpsY[0], psY[1], rpsY[1])
                    for e in range(NE):
                        self.dma_in(b1c[:, :], d["L1_ffn_b1"][e].rearrange("(c p) -> p c", p=128),
                                    writes=(rb1,), slow=True)
                        for mg in range(2):
                            for ml in range(16):
                                m = mg * 16 + ml
                                p = ml % 2
                                self.dma_in(w1b[p][:, :],
                                            d["w1e_b"][e][:, m * DC * 128:(m + 1) * DC * 128],
                                            writes=(rw1b[p],), chan=self.cW[p],
                                            extra_waits=tuple(R("mp", "dram").wt.items()))
                                for ns in range(2):
                                    pp = ns % 2
                                    for dd in range(DC):
                                        self.mm(rpsY[pp], psY[pp][:, :],
                                                w1b[p][:, dd * 128:(dd + 1) * 128],
                                                xn2[:, dd * HALF + ns * 512: dd * HALF + (ns + 1) * 512],
                                                (rw1b[p], rxn2), dd == 0, dd == DC - 1)
                                    em.op("scalar", "activation",
                                          hwin[:, ml * 1024 + ns * 512: ml * 1024 + (ns + 1) * 512],
                                          psY[pp][:, :], ACT.Silu, bias=b1c[:, m:m + 1],
                                          reads=(rpsY[pp], rb1), writes=(rhwin,))
                            for ttb in range(2):
                                for ml in range(16):
                                    m = mg * 16 + ml
                                    p = ml % 4
                                    self.dma_in(w2b[p][:, :], d["w2e_b"][e][m * 128:(m + 1) * 128, :],
                                                writes=(rw2b2[p],), chan=self.cW[2 + p],
                                                extra_waits=tuple(R("mp", "dram").wt.items()))
                                    for tl in range(4):
                                        for dh in range(2):
                                            self.mm(rpsY[tl * 2 + dh], psY[tl * 2 + dh][:, :],
                                                    hwin[:, ml * 1024 + (ttb * 4 + tl) * 128: ml * 1024 + (ttb * 4 + tl) * 128 + 128],
                                                    w2b[p][:, dh * 512:(dh + 1) * 512],
                                                    (rhwin, rw2b2[p]), ml == 0, ml == 15)
                                for tl in range(4):
                                    tt = half * 8 + ttb * 4 + tl
                                    for dh in range(2):
                                        em.op("vector", "scalar_tensor_tensor",
                                              acc[:, tt * D + dh * 512: tt * D + (dh + 1) * 512],
                                              psY[tl * 2 + dh][:, :],
                                              we[:, tt * NE + e: tt * NE + e + 1],
                                              acc[:, tt * D + dh * 512: tt * D + (dh + 1) * 512],
                                              ALU.mult, ALU.add,
                                              reads=(rpsY[tl * 2 + dh], rwe, racc), writes=(racc,))
                self.barrier()
            for tt in range(NTT):
                em.op("scalar", "activation", acc[:, tt * D:(tt + 1) * D],
                      acc[:, tt * D:(tt + 1) * D], ACT.Silu, reads=(racc,), writes=(racc,))
            self.tap("g2", acc)

            K1 = s1.enter_context(nc.sbuf_tensor("K1", [128, 2 * T], BF16))
            Q1 = s1.enter_context(nc.sbuf_tensor("Q1", [128, 2 * HALF], BF16))
            v_tm = s1.enter_context(nc.sbuf_tensor("v_tm", [128, NTT * D], BF16))
            rK1, rQ1, rvtm = R("K1"), R("Q1"), R("v_tm")
            for half in range(2):
                with ExitStack() as sp:
                    stg = sp.enter_context(nc.sbuf_tensor("p_stg", [128, D], F32))
                    wb = sp.enter_context(nc.sbuf_tensor("p_wb", [128, DC * D1], BF16))
                    wvub = sp.enter_context(nc.sbuf_tensor("p_wvub", [128, 2 * D], BF16))
                    V2 = sp.enter_context(nc.sbuf_tensor("p_V2", [128, 2 * HALF], BF16))
                    xnq = sp.enter_context(nc.sbuf_tensor("p_xnq", [128, DC * QB], BF16))
                    xnt = sp.enter_context(nc.sbuf_tensor("p_xnt", [128, D], F32))
                    psA = [sp.enter_context(nc.psum_tensor(f"p_psA{k}", [128, 512], F32)) for k in range(2)]
                    rstg, rwb, rwvub, rV2 = R("p_stg"), R("p_wb"), R("p_wvub"), R("p_V2")
                    rxnq, rxnt = R("p_xnq"), R("p_xnt")
                    rpsA = [R("p_psA", k) for k in range(2)]
                    for wname, dst_ap, dst_res, outcols in [
                        ("L1_attn_Wk", K1, rK1, "K"), ("L1_attn_Wq", Q1, rQ1, "Q"),
                        ("L1_attn_Wvdown", V2, rV2, "V2")]:
                        for dd2 in range(4):
                            self.dma_in(stg[:, :].rearrange("p (dd c) -> p dd c", c=D1)[:, 0:2, :],
                                        d[wname].rearrange("(dd p) c -> p dd c", p=128)[:, dd2 * 2:(dd2 + 1) * 2, :],
                                        writes=(rstg,))
                            em.op("vector", "tensor_copy",
                                  wb[:, dd2 * 2 * D1:(dd2 * 2 + 2) * D1], stg[:, 0:2 * D1],
                                  reads=(rstg,), writes=(rwb,))
                        for ql in range(2):
                            qb = half * 2 + ql
                            self.emit_xn_fm(qb * 4, 4, xnt, rxnt, xnq, rxnq, QB,
                                            psA[0], rpsA[0], psA[1], rpsA[1])
                            for mc in range(2):
                                pp = mc % 2
                                for dd in range(DC):
                                    self.mm(rpsA[pp], psA[pp][:, :],
                                            wb[:, dd * D1 + mc * 128: dd * D1 + (mc + 1) * 128],
                                            xnq[:, dd * QB:(dd + 1) * QB],
                                            (rwb, rxnq), dd == 0, dd == DC - 1)
                                if outcols == "K":
                                    em.op("scalar", "copy",
                                          dst_ap[:, mc * T + qb * 512: mc * T + (qb + 1) * 512],
                                          psA[pp][:, :], reads=(rpsA[pp],), writes=(dst_res,))
                                elif outcols == "Q":
                                    em.op("scalar", "activation",
                                          dst_ap[:, mc * HALF + ql * 512: mc * HALF + (ql + 1) * 512],
                                          psA[pp][:, :], ACT.Copy, scale=0.0625,
                                          reads=(rpsA[pp],), writes=(dst_res,))
                                else:
                                    em.op("scalar", "copy",
                                          dst_ap[:, mc * HALF + ql * 512: mc * HALF + (ql + 1) * 512],
                                          psA[pp][:, :], reads=(rpsA[pp],), writes=(dst_res,))
                    for c2 in range(2):
                        self.dma_in(stg[:, 0:D].rearrange("p (c v) -> p c v", c=1),
                                    d["L1_attn_Wvup"].rearrange("(c p) v -> p c v", p=128)[:, c2:c2 + 1, :],
                                    writes=(rstg,))
                        em.op("vector", "tensor_copy", wvub[:, c2 * D:(c2 + 1) * D], stg[:, 0:D],
                              reads=(rstg,), writes=(rwvub,))
                    for tl in range(8):
                        tt = half * 8 + tl
                        for bk in range(2):
                            pp = bk % 2
                            for c2 in range(2):
                                self.mm(rpsA[pp], psA[pp][:, :],
                                        V2[:, c2 * HALF + tl * 128: c2 * HALF + (tl + 1) * 128],
                                        wvub[:, c2 * D + bk * 512: c2 * D + (bk + 1) * 512],
                                        (rV2, rwvub), c2 == 0, c2 == 1)
                            em.op("scalar", "copy",
                                  v_tm[:, tt * D + bk * 512: tt * D + (bk + 1) * 512],
                                  psA[pp][:, :], reads=(rpsA[pp],), writes=(rvtm,))
                    self.barrier()
                with ExitStack() as sb:
                    pT1 = sb.enter_context(nc.sbuf_tensor("t_pT", [128, 16 * 512], BF16))
                    zr = sb.enter_context(nc.sbuf_tensor("t_zr", [1, 512], F32))
                    zc = sb.enter_context(nc.sbuf_tensor("t_zc", [128, 4], F32))
                    rzc2 = sb.enter_context(nc.sbuf_tensor("t_rzc", [128, 4], F32))
                    ub = sb.enter_context(nc.sbuf_tensor("t_ub", [128, 512], F32))
                    psS = [sb.enter_context(nc.psum_tensor(f"t_psS{k}", [128, 512], F32)) for k in range(2)]
                    psZ = sb.enter_context(nc.psum_tensor("t_psZ", [128, 512], F32))
                    psA2 = [sb.enter_context(nc.psum_tensor(f"t_psA{k}", [128, 512], F32)) for k in range(2)]
                    rpT1, rzr, rzc, rrzc, rub = R("t_pT"), R("t_zr"), R("t_zc"), R("t_rzc"), R("t_ub")
                    rpsS = [R("t_psS", k) for k in range(2)]
                    rpsZ = R("t_psZ")
                    rpsA2 = [R("t_psA", k) for k in range(2)]
                    for ql in range(2):
                        qb = half * 2 + ql
                        nkc = qb * 4 + 4
                        for kc in range(nkc):
                            ss = kc % 2
                            for c2 in range(2):
                                self.mm(rpsS[ss], psS[ss][:, :],
                                        K1[:, c2 * T + kc * 128:(c2 * T) + (kc + 1) * 128],
                                        Q1[:, c2 * HALF + ql * 512: c2 * HALF + (ql + 1) * 512],
                                        (rK1, rQ1), c2 == 0, c2 == 1)
                            em.op("scalar", "activation", pT1[:, kc * 512:(kc + 1) * 512],
                                  psS[ss][:, :], ACT.Exp, reads=(rpsS[ss],), writes=(rpT1,))
                            j = kc - qb * 4
                            if j >= 0:
                                em.op("vector", "tensor_mul", pT1[:, kc * 512:(kc + 1) * 512],
                                      pT1[:, kc * 512:(kc + 1) * 512],
                                      self.masks[:, j * 512:(j + 1) * 512],
                                      reads=(rpT1, R("masks")), writes=(rpT1,))
                            self.mm(rpsZ, psZ[0:1, :], self.ones_col_bf[:, 0:1],
                                    pT1[:, kc * 512:(kc + 1) * 512],
                                    (R("ones"), rpT1), kc == 0, kc == nkc - 1)
                        em.op("vector", "tensor_copy", zr[0:1, :], psZ[0:1, :],
                              reads=(rpsZ,), writes=(rzr,))
                        for s in range(4):
                            em.op("tensor", "transpose", psZ[:, 0:1],
                                  zr[0:1, s * 128:(s + 1) * 128], self.identf[0:1, 0:1],
                                  reads=(rzr, R("ident")), writes=(rpsZ,))
                            em.op("vector", "tensor_copy", zc[:, s:s + 1], psZ[:, 0:1],
                                  reads=(rpsZ,), writes=(rzc,))
                        em.op("vector", "reciprocal", rzc2[:, :], zc[:, :],
                              reads=(rzc,), writes=(rrzc,))
                        for qs in range(4):
                            tt = qb * 4 + qs
                            for dh in range(2):
                                pa = dh % 2
                                for kc in range(nkc):
                                    self.mm(rpsA2[pa], psA2[pa][:, :],
                                            pT1[:, kc * 512 + qs * 128: kc * 512 + (qs + 1) * 128],
                                            v_tm[:, kc * D + dh * 512: kc * D + (dh + 1) * 512],
                                            (rpT1, rvtm), kc == 0, kc == nkc - 1)
                                em.op("vector", "scalar_tensor_tensor", ub[:, :],
                                      psA2[pa][:, :], rzc2[:, qs:qs + 1],
                                      acc[:, tt * D + dh * 512: tt * D + (dh + 1) * 512],
                                      ALU.mult, ALU.mult,
                                      reads=(rpsA2[pa], rrzc, racc), writes=(rub,))
                                em.op("vector", "tensor_add",
                                      self.x[:, tt * D + dh * 512: tt * D + (dh + 1) * 512],
                                      self.x[:, tt * D + dh * 512: tt * D + (dh + 1) * 512],
                                      ub[:, :], reads=(rub, R("x")), writes=(R("x"),))
                    self.barrier()

    def tap(self, name, sbuf_t, rows=128):
        if name not in self.debug_taps:
            return
        nc, em = self.nc, self.em
        ap = nc.dram_tensor(f"tap_{name}", [rows, sbuf_t.shape[1]], sbuf_t.dtype,
                            kind="ExternalOutput").ap()
        self.tap_aps[name] = ap
        self.dma_out(ap[:, :], sbuf_t[0:rows, :], reads=(),
                     extra_waits=em.barrier_tok())
        em.wait_toks("gpsimd", [(self.cG, self.cG.n)])
        em.flush()

    def phase_out(self):
        d, em = self.d, self.em
        oout = d["out"].rearrange("(n p) d -> p n d", p=128)
        xsrc = self.x[:, :].rearrange("p (n d) -> p n d", d=D)
        self.dma_out(oout, xsrc, reads=(self.R("x"),), extra_waits=em.barrier_tok())
        em.wait_toks("gpsimd", [(self.cG, self.cG.n)])
        em.flush()


PAIR_PERM = np.concatenate([
    128 * c + 64 * h + np.concatenate([np.arange(0, 64, 2), np.arange(1, 64, 2)])
    for c in range(8) for h in range(2)])


def _derived_inputs(inputs):
    m = {}
    m["Wq_p"] = np.ascontiguousarray(np.asarray(inputs["L0_attn_Wq"])[PAIR_PERM])
    m["Wk_p"] = np.ascontiguousarray(np.asarray(inputs["L0_attn_Wk"])[PAIR_PERM])
    m["bq_p"] = np.ascontiguousarray(np.asarray(inputs["L0_attn_bq"])[PAIR_PERM])
    m["bk_p"] = np.ascontiguousarray(np.asarray(inputs["L0_attn_bk"])[PAIR_PERM])
    rw1 = np.asarray(inputs["L1_ffn_rw1"], np.float32)
    m["rw1_p"] = np.ascontiguousarray(
        rw1.reshape(8, 128, 32, 128).transpose(1, 2, 0, 3).reshape(128, 32 * 8 * 128))
    w1e = np.asarray(inputs["L1_ffn_w1"], np.float32)
    m["w1e_p"] = np.ascontiguousarray(
        w1e.reshape(8, 8, 128, 32, 128).transpose(0, 2, 3, 1, 4).reshape(8, 128, 32 * 8 * 128))
    return m


_CONSTS = None


def kernel(**inputs):
    global _CONSTS
    if _CONSTS is None:
        _CONSTS = _host_consts()
    mk = MK()
    nc = mk.build()
    xs = np.ascontiguousarray(inputs["L0_attn_x"])
    base = {k: np.ascontiguousarray(v) for k, v in inputs.items()
            if k not in ("L0_attn_x", "L0_attn_token_positions")}
    base.update(_CONSTS)
    base.update(_derived_inputs(inputs))
    in_maps = []
    for c in range(8):
        m = dict(base)
        m["x"] = xs[c]
        in_maps.append(m)
    res = run_bass_kernel_spmd(nc, in_maps, core_ids=list(range(8)))
    out = np.stack([np.asarray(r["out"]) for r in res.results], axis=0)
    return out.astype(np.float32)
